# version 17 (final)
# baseline (speedup 1.0000x reference)
import numpy as np
import ml_dtypes
import concourse.bass as bass
import concourse.mybir as mybir
from concourse.bass_utils import run_bass_kernel_spmd
from contextlib import ExitStack

F32 = mybir.dt.float32
BF16 = mybir.dt.bfloat16
ALU = mybir.AluOpType
ACT = mybir.ActivationFunctionType
AX = mybir.AxisListType.X

MAGIC = float(np.float32(1.5 * 2.0 ** 23))
NEG = -30000.0
SEQS = 2
S = 4096
D = 1024
INNER = 4096
NT = S // 128
NSUP = S // 512
EPS = 1e-5

ENGS = ("sync", "gpsimd", "tensor", "vector", "scalar")


def _finish(nc, ctx, block_emit):
    block_emit()
    ctx.close()
    return nc


def build_program(debug=False, stage=6):
    nc = bass.Bass("TRN2", target_bir_lowering=False, debug=False,
                   detect_race_conditions=False)

    x_d = nc.dram_tensor("x", [SEQS, S, D], F32, kind="ExternalInput").ap()
    wq_d = nc.dram_tensor("wq", [D, D], F32, kind="ExternalInput").ap()
    wk_d = nc.dram_tensor("wk", [D, D], F32, kind="ExternalInput").ap()
    wv_d = nc.dram_tensor("wv", [D, D], F32, kind="ExternalInput").ap()
    bq_d = nc.dram_tensor("bq", [D], F32, kind="ExternalInput").ap()
    bk_d = nc.dram_tensor("bk", [D], F32, kind="ExternalInput").ap()
    bv_d = nc.dram_tensor("bv", [D], F32, kind="ExternalInput").ap()
    wg_d = nc.dram_tensor("wg", [INNER, D], F32, kind="ExternalInput").ap()
    wu_d = nc.dram_tensor("wu", [INNER, D], F32, kind="ExternalInput").ap()
    wd_d = nc.dram_tensor("wd", [D, INNER], F32, kind="ExternalInput").ap()
    id_d = nc.dram_tensor("ident", [128, 128], BF16, kind="ExternalInput").ap()
    idf_d = nc.dram_tensor("identf", [128, 128], F32, kind="ExternalInput").ap()
    mask_d = nc.dram_tensor("mask", [128, 128], F32, kind="ExternalInput").ap()
    magic_d = nc.dram_tensor("magicv", [128, 1], F32, kind="ExternalInput").ap()
    out_d = nc.dram_tensor("out", [SEQS, S, D], F32, kind="ExternalOutput").ap()

    wqkvT_s = nc.dram_tensor("wqkvT_s", [3, 8, 128, D], BF16, kind="Internal").ap()
    wguT_s = nc.dram_tensor("wguT_s", [2, 8, 128, INNER], BF16, kind="Internal").ap()
    wdT_s = nc.dram_tensor("wdT_s", [32, 128, D], BF16, kind="Internal").ap()
    z_sd = nc.dram_tensor("z_s", [NT, 128, D], F32, kind="Internal").ap()
    pint_s = nc.dram_tensor("pint_s", [NT, 128, INNER], BF16, kind="Internal").ap()
    mw_s = nc.dram_tensor("mw_s", [4], F32, kind="Internal").ap()

    dbg = {}
    if debug:
        dbg["kt"] = nc.dram_tensor("dbg_kt", [8, 128, S], BF16, kind="ExternalOutput").ap()
        dbg["v"] = nc.dram_tensor("dbg_v", [NT, 128, D], BF16, kind="ExternalOutput").ap()
        dbg["z"] = nc.dram_tensor("dbg_z", [NT, 128, D], F32, kind="ExternalOutput").ap()
        dbg["wgT"] = nc.dram_tensor("dbg_wgT", [8, 128, INNER], BF16, kind="ExternalOutput").ap()
        dbg["wdT"] = nc.dram_tensor("dbg_wdT", [32, 128, D], BF16, kind="ExternalOutput").ap()
        dbg["mw"] = nc.dram_tensor("dbg_mw", [4], F32, kind="ExternalOutput").ap()
        dbg["pint"] = nc.dram_tensor("dbg_pint", [NT, 128, INNER], BF16, kind="ExternalOutput").ap()

    ops = {e: [] for e in ENGS}
    waited = {e: {} for e in ENGS}

    def emit(eng, fn):
        ops[eng].append(fn)

    class Sem:
        __slots__ = ("name", "n", "h")

        def __init__(self, name):
            self.name, self.n, self.h = name, 0, None

    st_ = Sem("st")
    sv = Sem("sv")
    ss = Sem("ss")
    gsem = Sem("gs")
    allsems = [st_, sv, ss, gsem]

    class DmaQ:

        def __init__(self, name, k):
            self.sems = [Sem(f"{name}{i}") for i in range(k)]
            allsems.extend(self.sems)
            self.i = 0

        def inc(self, eng, fn):
            s_ = self.sems[self.i % len(self.sems)]
            self.i += 1
            wait(eng, s_, s_.n)
            s_.n += 16
            g = lambda e, fn=fn, s_=s_: fn(e).then_inc(s_.h, 16)
            g._has_inc = True
            emit(eng, g)
            return (s_, s_.n)

        def bar(self):
            return [(s_, s_.n) for s_ in self.sems if s_.n > 0]

    def wait(eng, s_, val):
        if val <= 0:
            return
        if waited[eng].get(s_.name, 0) >= val:
            return
        waited[eng][s_.name] = val
        g = lambda e, s_=s_, val=val: e.wait_ge(s_.h, val)
        g._is_wait = True
        emit(eng, g)

    def inc(eng, s_, fn, n=1):
        s_.n += n
        g = lambda e, fn=fn, s_=s_, n=n: fn(e).then_inc(s_.h, n)
        g._has_inc = True
        emit(eng, g)
        return s_.n

    def run(eng, fn):
        emit(eng, fn)

    def vsync():
        f = ops["vector"][-1]
        assert not getattr(f, "_is_wait", False), "vsync after wait"
        if not getattr(f, "_has_inc", False):
            mark("vector", sv)
        wait("vector", sv, sv.n)

    def wait_t(eng, tkt):
        if tkt:
            wait(eng, tkt[0], tkt[1])

    def wait_ts(eng, tkts):
        best = {}
        for t in tkts:
            if t and (t[0].name not in best or best[t[0].name][1] < t[1]):
                best[t[0].name] = t
        for t in best.values():
            wait(eng, t[0], t[1])

    qld = DmaQ("ld", 8)
    qgl = DmaQ("gl", 4)
    qso = DmaQ("so", 8)

    def drain_queues():
        wait_ts("sync", qld.bar())
        wait_ts("gpsimd", qgl.bar())
        wait_ts("scalar", qso.bar())

    def mark(eng, s_, n=1):
        s_.n += n
        f = ops[eng].pop()
        assert not getattr(f, "_has_inc", False), f"double inc on {eng}"
        g = lambda e, f=f, s_=s_, n=n: f(e).then_inc(s_.h, n)
        g._has_inc = True
        ops[eng].append(g)
        return s_.n

    ctx = ExitStack()
    sbuf = lambda name, shape, dt: ctx.enter_context(nc.sbuf_tensor(name, shape, dt))
    psum = lambda name, shape, dt: ctx.enter_context(nc.psum_tensor(name, shape, dt))

    arena = sbuf("arena", [128, 65536], BF16)
    wbuf = sbuf("wbuf", [128, 16384], BF16)
    xbuf = sbuf("xbuf", [128, 4096], F32)
    xt = sbuf("xt", [128, 4096], BF16)
    qt = sbuf("qt", [128, 4096], BF16)
    zbuf = sbuf("zbuf", [128, 2048], F32)
    htb = sbuf("htb", [128, 1024], F32)
    id_sb = sbuf("id_sb", [128, 128], BF16)
    idf_sb = sbuf("idf_sb", [128, 128], F32)
    mask_sb = sbuf("mask_sb", [128, 128], F32)
    stat = sbuf("stat", [128, 224], F32)
    M1C, V1C, R1C, GAMC, CMAXC, LPARTC, MISC = 0, 32, 64, 96, 128, 136, 144
    BQC, BKC = 192, 200
    MAGICC = 208

    s_ps = psum("s_ps", [128, 3072], F32)
    z_ps = psum("z_ps", [128, 1024], F32)
    pt_ps = s_ps.bitcast(BF16)[:, 5120:6144]

    for s2 in allsems:
        s2.h = ctx.enter_context(nc.semaphore(s2.name))

    KT = lambda kc: arena[:, kc * 4096:(kc + 1) * 4096]
    Vt = lambda vt: arena[:, 32768 + vt * 1024: 32768 + (vt + 1) * 1024]
    WGT = lambda dc: arena[:, dc * 4096:(dc + 1) * 4096]
    WUT = lambda dc: arena[:, 32768 + dc * 4096: 32768 + (dc + 1) * 4096]
    WDT = lambda ic: arena[:, ic * 1024:(ic + 1) * 1024]
    WAT = lambda slot, dc: wbuf[:, slot * 8192 + dc * 1024: slot * 8192 + (dc + 1) * 1024]
    PTLD = lambda b: wbuf[:, b * 4096:(b + 1) * 4096]
    HINT = wbuf[:, 8192:9216]
    HTT = lambda dc: wbuf[:, 9216 + dc * 128: 9216 + (dc + 1) * 128]
    PSTG = wbuf[:, 10240:14336]
    XSUP = lambda tt2: xbuf[:, tt2 * 1024:(tt2 + 1) * 1024]
    PBUF = xbuf[:, 0:4096]
    OBUF = lambda b: qt.bitcast(F32)[:, b * 1024:(b + 1) * 1024]
    XT = lambda dc: xt[:, dc * 512:(dc + 1) * 512]
    PSL = lambda b: xt[:, b * 512:(b + 1) * 512]
    PTSL = lambda b: xt[:, 1024 + b * 512: 1024 + (b + 1) * 512]
    PINT = xt[:, 0:4096]
    QT = lambda kc: qt[:, kc * 512:(kc + 1) * 512]
    QSTG = qt[:, 0:4096]
    ZSL = lambda b: zbuf[:, b * 1024:(b + 1) * 1024]
    T1 = lambda b: htb[:, b * 512:(b + 1) * 512]
    HT_ = htb[:, 0:1024]

    stc = lambda c, w=1: stat[:, c:c + w]

    def block_emit():
        block = ctx.enter_context(nc.Block())

        def mk(engname):
            def body(e):
                for fn in ops[engname]:
                    fn(e)
            return body
        block.sync(mk("sync"))
        block.gpsimd(mk("gpsimd"))
        block.tensor(mk("tensor"))
        block.vector(mk("vector"))
        block.scalar(mk("scalar"))

    def tx_bf16(src_ap, slot):
        run("tensor", lambda e, a=src_ap, s=slot: nc.tensor.transpose(
            pt_ps[:, s * 128:(s + 1) * 128], a, id_sb[:]))

    XPSB = (s_ps[:, 2048:2560], s_ps[:, 2560:3072])

    def tx_f32(src_ap, slot, bank=0):
        run("tensor", lambda e, a=src_ap, s=slot, bank=bank: nc.tensor.transpose(
            XPSB[bank][:, s * 128:(s + 1) * 128], a, idf_sb[:]))

    t_consts = [
        qld.inc("sync", lambda e: e.dma_start(out=id_sb[:], in_=id_d[:])),
        qld.inc("sync", lambda e: e.dma_start(out=idf_sb[:], in_=idf_d[:])),
        qld.inc("sync", lambda e: e.dma_start(out=mask_sb[:], in_=mask_d[:])),
        qld.inc("sync", lambda e: e.dma_start(out=stat[:, MAGICC:MAGICC + 1], in_=magic_d[:])),
    ]
    wait_ts("scalar", t_consts)
    bias_tkts = [
        qgl.inc("gpsimd", lambda e: e.dma_start(
            out=stat[:, BQC:BQC + 8], in_=bq_d.rearrange("(c p) -> p c", p=128),
            allow_slow_non_contiguous=True)),
        qgl.inc("gpsimd", lambda e: e.dma_start(
            out=stat[:, BKC:BKC + 8], in_=bk_d.rearrange("(c p) -> p c", p=128),
            allow_slow_non_contiguous=True)),
    ]
    wait_ts("tensor", t_consts)

    if stage == 11:
        ops_done = True
    qkv_txdone = {}
    for wi, w_src in enumerate((wq_d, wk_d, wv_d) if (stage >= 12 or stage < 11) else ()):
        w_r = w_src.rearrange("(t p) d -> t p d", p=128)
        prev_store = store_tkts if wi > 0 else []
        half_tags = {}
        for half in range(2):
            base = (half % 2) * 4096
            wait("gpsimd", st_, qkv_txdone.get((wi - 1, half), 0))
            tags = []
            for j in range(4):
                kt_i = half * 4 + j
                tags.append(qgl.inc("gpsimd", lambda e, kt_i=kt_i, j=j, w_r=w_r, base=base:
                                    e.dma_start(out=wbuf[:, base + j * 1024: base + (j + 1) * 1024],
                                                in_=w_r[kt_i])))
            half_tags[half] = tags
        for half in range(2):
            base = (half % 2) * 4096
            wait_ts("tensor", half_tags[half])
            wait_ts("tensor", prev_store)
            for dc in range(8):
                for j in range(4):
                    tx_bf16(wbuf[:, base + j * 1024 + dc * 128: base + j * 1024 + dc * 128 + 128], j)
                ttag = mark("tensor", st_)
                qkv_txdone[(wi, half)] = ttag
                wait("vector", st_, ttag)
                vtag = inc("vector", sv, lambda e, dc=dc, half=half: e.tensor_copy(
                    wbuf[:, 8192 + dc * 1024 + half * 512: 8192 + dc * 1024 + half * 512 + 512],
                    pt_ps[:, 0:512]))
                wait("tensor", sv, vtag)
        vdone = sv.n
        wait("scalar", sv, vdone)
        store_tkts = []
        for dc in range(8):
            store_tkts.append(qso.inc("scalar", lambda e, wi=wi, dc=dc: e.dma_start(
                out=wqkvT_s[wi, dc], in_=wbuf[:, 8192 + dc * 1024: 8192 + (dc + 1) * 1024])))
        wait_ts("gpsimd", store_tkts)

    ACCG, ACCU, ACCD, RED1, ONESC = 176, 177, 178, 179, 180
    run("vector", lambda e: e.memset(stc(ONESC), 1.0))

    def mean_pass(w_src, ntiles, width, acc_col):
        flat = w_src.rearrange("(t p) d -> p t d", p=128)
        nch = ntiles * width // 2048
        red_tags = {}
        prev_v = sv.n
        prev_s = ss.n
        slots = [xbuf[:, 0:2048], xbuf[:, 2048:4096], zbuf[:, 0:2048]]
        qmap = [("sync", qld), ("scalar", qso), ("gpsimd", qgl)]

        def chunk_ap(k):
            if width == 1024:
                return flat[:, 2 * k:2 * k + 2, :]
            return flat[:, k // 2, (k % 2) * 2048:(k % 2) * 2048 + 2048]

        def load(k):
            eng, q = qmap[k % 3]
            wait(eng, sv, red_tags.get(k - 3, prev_v))
            wait(eng, ss, prev_s)
            return q.inc(eng, lambda e, k=k:
                         e.dma_start(out=slots[k % 3], in_=chunk_ap(k)))

        ld_tags = {j: load(j) for j in range(min(3, nch))}
        for k in range(nch):
            wait_t("vector", ld_tags[k])
            red_tags[k] = inc("vector", sv, lambda e, k=k: e.reduce_sum(
                stc(MISC + k), slots[k % 3], AX,
                apply_absolute_value=True))
            if k + 3 < nch:
                ld_tags[k + 3] = load(k + 3)
        vsync()
        inc("vector", sv, lambda e, acc_col=acc_col, nch=nch: e.reduce_sum(
            stc(acc_col), stat[:, MISC:MISC + nch], AX))
        return sv.n

    if stage >= 13 or stage < 11:
        mean_pass(wg_d, 32, 1024, ACCG)
        mean_pass(wu_d, 32, 1024, ACCU)
        mean_pass(wd_d, 8, 4096, ACCD)
    wait_ts("gpsimd", store_tkts)
    wait("gpsimd", st_, st_.n)
    wkv_pref = []
    for wi, slot in ((1, 0), (2, 1)):
        for dc in range(8):
            wkv_pref.append(qgl.inc("gpsimd", lambda e, wi=wi, slot=slot, dc=dc:
                            e.dma_start(out=WAT(slot, dc), in_=wqkvT_s[wi, dc])))
    wait("tensor", sv, sv.n)
    wait("tensor", gsem, gsem.n)
    if 11 <= stage < 14:
        drain_queues()
        return _finish(nc, ctx, block_emit)
    run("tensor", lambda e: nc.tensor.matmul(
        z_ps[0:3, 0:1], stat[:, ACCG:ACCG + 3], stc(ONESC), start=True, stop=True))
    ttag = mark("tensor", st_)
    wait("scalar", st_, ttag)
    run("scalar", lambda e: e.activation(stat[0:3, RED1:RED1 + 1], z_ps[0:3, 0:1], ACT.Copy))
    stag = mark("scalar", ss)
    wait("scalar", ss, stag)
    mwtag = qso.inc("scalar", lambda e: e.dma_start(
        out=mw_s[0:3], in_=stat[0:3, RED1:RED1 + 1]))
    wait_t("sync", mwtag)
    MGC = 209
    SGC = 212
    mwld = qld.inc("sync", lambda e: e.dma_start(
        out=stat[:, MGC:MGC + 3], in_=mw_s[None, 0:3].to_broadcast([128, 3])))
    wait_t("vector", mwld)
    run("vector", lambda e: e.tensor_scalar(
        stat[:, MGC:MGC + 3], stat[:, MGC:MGC + 3], 1.0 / (INNER * D), 1e-5,
        ALU.mult, ALU.max))
    vsync()
    run("vector", lambda e: e.reciprocal(stat[:, SGC:SGC + 3], stat[:, MGC:MGC + 3]))
    vtag_sw = mark("vector", sv)
    wait("scalar", sv, vtag_sw)
    wait("tensor", sv, vtag_sw)

    def quant_pass(w_src, ntiles, width, s_col, arena_off, chunks_are_dc):
        flat = w_src.rearrange("(t p) d -> p t d", p=128)
        nch = ntiles * width // 2048
        arena_v = (arena[:, arena_off:arena_off + 32768]
                   .rearrange("p (dc c) -> p dc c", dc=8 if chunks_are_dc else 32))
        red_tags = {}
        qv_tags = {}
        prev_v = sv.n
        prev_s = ss.n
        stg_free = {0: 0, 1: 0}

        def chunk_ap(k):
            if width == 1024:
                return flat[:, 2 * k:2 * k + 2, :]
            return flat[:, k // 2, (k % 2) * 2048:(k % 2) * 2048 + 2048]

        def load(k):
            half = (k % 2) * 2048
            wait("sync", ss, red_tags.get(k - 2, prev_s))
            wait("sync", sv, prev_v)
            return qld.inc("sync", lambda e, k=k, half=half:
                           e.dma_start(out=xbuf[:, half:half + 2048],
                                       in_=chunk_ap(k)))

        def quant(k):
            half = (k % 2) * 2048
            wait_ts("scalar", [ld_tags[k]])
            last_sp = 0
            for piece in range(2):
                stg = htb[:, 0:1024] if piece == 0 else ZSL(0)
                wait("scalar", sv, stg_free[piece])
                sp = inc("scalar", ss, lambda e, half=half, piece=piece, stg=stg,
                         s_col=s_col: e.activation(
                             stg, xbuf[:, half + piece * 1024: half + (piece + 1) * 1024],
                             ACT.Identity, bias=stc(MAGICC), scale=stc(s_col)))
                last_sp = sp
                wait("vector", ss, sp)
                run("vector", lambda e, stg=stg: e.tensor_scalar(
                    stg, stg, -MAGIC, 1.0, ALU.add, ALU.min))
                vsync()
                stg_free[piece] = inc("vector", sv, lambda e, stg=stg, k=k,
                                      piece=piece: e.tensor_scalar(
                                          qt[:, (k % 2) * 2048 + piece * 1024:
                                             (k % 2) * 2048 + (piece + 1) * 1024],
                                          stg, -1.0, 1.0, ALU.max, ALU.mult))
            red_tags[k] = last_sp
            qv_tags[k] = stg_free[1]

        def txc(k):
            qhalf = (k % 2) * 2048
            wait("tensor", sv, qv_tags[k])
            for g in range(2):
                if width == 1024:
                    t_i = 2 * k + g
                    dst = arena_v[:, :, t_i * 128:(t_i + 1) * 128]
                else:
                    t_i = k // 2
                    ic0 = (k % 2) * 16 + g * 8
                    dst = arena_v[:, ic0:ic0 + 8, t_i * 128:(t_i + 1) * 128]
                for cc in range(8):
                    tx_bf16(qt[:, qhalf + g * 1024 + cc * 128:
                               qhalf + g * 1024 + cc * 128 + 128], cc)
                ttag = mark("tensor", st_)
                wait("vector", st_, ttag)
                ev = inc("vector", sv, lambda e, dst=dst: e.tensor_copy(
                    dst, pt_ps[:, 0:1024]))
                wait("tensor", sv, ev)

        ld_tags = {0: load(0), 1: load(1)}
        for k in range(nch):
            quant(k)
            if k + 2 < nch:
                ld_tags[k + 2] = load(k + 2)
            if k > 0:
                txc(k - 1)
        txc(nch - 1)

    quant_pass(wg_d, 32, 1024, SGC, 0, True)
    wait("scalar", sv, sv.n)
    wg_stored = []
    for dc in range(8):
        wg_stored.append(qso.inc("scalar", lambda e, dc=dc: e.dma_start(
            out=wguT_s[0, dc], in_=arena[:, dc * 4096:(dc + 1) * 4096])))
    quant_pass(wu_d, 32, 1024, SGC + 1, 32768, True)
    wait("scalar", sv, sv.n)
    for dc in range(8):
        wg_stored.append(qso.inc("scalar", lambda e, dc=dc: e.dma_start(
            out=wguT_s[1, dc], in_=arena[:, 32768 + dc * 4096: 32768 + (dc + 1) * 4096])))
    wait_ts("tensor", wg_stored[:8])
    quant_pass(wd_d, 8, 4096, SGC + 2, 0, False)
    wait("scalar", sv, sv.n)
    wd_stored = []
    for ic0 in range(0, 32, 8):
        wd_stored.append(qso.inc("scalar", lambda e, ic0=ic0: e.dma_start(
            out=wdT_s[ic0:ic0 + 8], in_=arena[:, ic0 * 1024:(ic0 + 8) * 1024])))
    x0_pre = {}
    wait("sync", ss, ss.n)
    x0_pre[0] = qld.inc("sync", lambda e: e.dma_start(
        out=xbuf.rearrange("p (q d) -> p q d", q=4),
        in_=x_d[0].rearrange("(u q p) d -> u p q d", p=128, q=4)[0]))
    if debug:
        wait_ts("scalar", wd_stored)
        for dc in range(8):
            qso.inc("scalar", lambda e, dc=dc: e.dma_start(
                out=dbg["wgT"][dc], in_=wguT_s[0, dc]))
        for ic0 in range(0, 32, 8):
            qso.inc("scalar", lambda e, ic0=ic0: e.dma_start(
                out=dbg["wdT"][ic0:ic0 + 8], in_=wdT_s[ic0:ic0 + 8]))
        qso.inc("scalar", lambda e: e.dma_start(out=dbg["mw"][0:3], in_=mw_s[0:3]))

    score_mark = [0]
    for sq in range(SEQS):
        if sq == 1 and stage < 6:
            break
        x_sup = x_d[sq].rearrange("(u q p) d -> u p q d", p=128, q=4)
        out_seq = out_d[sq].rearrange("(t p) d -> t p d", p=128)

        wait("tensor", sv, sv.n)
        wait("tensor", ss, ss.n)
        wait_ts("sync", qso.bar())
        wait("sync", st_, st_.n)
        wait_ts("scalar", qld.bar())
        wait_ts("scalar", qso.bar())
        wait_ts("scalar", bias_tkts)

        if sq == 0:
            wkv_done = wkv_pref
        else:
            wkv_done = []
            for wi, slot in ((1, 0), (2, 1)):
                for dc in range(8):
                    wkv_done.append(qld.inc("sync", lambda e, wi=wi, slot=slot, dc=dc:
                                    e.dma_start(out=WAT(slot, dc), in_=wqkvT_s[wi, dc])))

        if stage < 2 or stage >= 11:
            break
        if sq == 0:
            wait_ts("tensor", wd_stored)
        kv_free = {b: 0 for b in range(4)}
        v_free = {0: 0, 1: 0}
        xt_free_t = 0
        xb_free_t = 0
        for sup in range(NSUP):
            if sup == 0 and sq in x0_pre:
                xtag = x0_pre[sq]
            else:
                wait("sync", st_, xb_free_t)
                xtag = qld.inc("sync", lambda e, sup=sup, x_sup=x_sup: e.dma_start(
                    out=xbuf.rearrange("p (q d) -> p q d", q=4), in_=x_sup[sup]))
            wait_t("tensor", xtag)
            wait("tensor", st_, xt_free_t)
            ev_tag = {0: 0, 1: 0}
            for dc in range(8):
                bk = dc % 2
                wait("tensor", ss, ev_tag[bk])
                for tt2 in range(4):
                    tx_f32(xbuf[:, tt2 * 1024 + dc * 128: tt2 * 1024 + dc * 128 + 128],
                           tt2, bk)
                ttag = mark("tensor", st_)
                wait("scalar", st_, ttag)
                ev_tag[bk] = inc("scalar", ss, lambda e, dc=dc, bk=bk: e.activation(
                    XT(dc), XPSB[bk][:], ACT.Copy))
            wait("tensor", ss, ev_tag[1])
            xb_free_t = ttag
            wait_ts("tensor", wkv_done)
            for kc in range(8):
                b = kc % 4
                wait("tensor", ss, kv_free[b])
                for dc in range(8):
                    run("tensor", lambda e, dc=dc, kc=kc, b=b: nc.tensor.matmul(
                        s_ps[:, b * 512:(b + 1) * 512],
                        WAT(0, dc)[:, kc * 128:(kc + 1) * 128],
                        XT(dc), start=(dc == 0), stop=(dc == 7)))
                ktag = mark("tensor", st_)
                wait("scalar", st_, ktag)
                kv_free[b] = inc("scalar", ss, lambda e, kc=kc, sup=sup, b=b: e.activation(
                    KT(kc)[:, sup * 512:(sup + 1) * 512], s_ps[:, b * 512:(b + 1) * 512],
                    ACT.Identity, bias=stat[:, BKC + kc:BKC + 1 + kc]))
            for tt2 in range(4):
                for half in range(2):
                    wait("tensor", sv, v_free[half])
                    for dc in range(8):
                        run("tensor", lambda e, dc=dc, tt2=tt2, half=half: nc.tensor.matmul(
                            z_ps[:, half * 512:(half + 1) * 512],
                            XT(dc)[:, tt2 * 128:(tt2 + 1) * 128],
                            WAT(1, dc)[:, half * 512:(half + 1) * 512],
                            start=(dc == 0), stop=(dc == 7)))
                    vtag_t = mark("tensor", st_)
                    wait("vector", st_, vtag_t)
                    v_free[half] = inc("vector", sv, lambda e, tt2=tt2, half=half, sup=sup:
                                       e.tensor_copy(
                                           Vt(sup * 4 + tt2)[:, half * 512:(half + 1) * 512],
                                           z_ps[:, half * 512:(half + 1) * 512]))
            xt_free_t = vtag_t

        if debug and sq == 0:
            wait("scalar", sv, sv.n)
            wait("scalar", ss, ss.n)
            for kc in range(8):
                qso.inc("scalar", lambda e, kc=kc: e.dma_start(
                    out=dbg["kt"][kc], in_=KT(kc)))
            for vt in range(NT):
                qso.inc("scalar", lambda e, vt=vt: e.dma_start(
                    out=dbg["v"][vt], in_=Vt(vt)))

        if stage < 3:
            break
        wait("sync", st_, st_.n)
        wq_done = []
        for dc in range(8):
            wq_done.append(qld.inc("sync", lambda e, dc=dc:
                           e.dma_start(out=WAT(0, dc), in_=wqkvT_s[0, dc])))
        wait_ts("tensor", wq_done)

        zsl_free = {0: (None, 0, 0), 1: (None, 0, 0)}
        xb_free = {0: (0, xb_free_t, 0), 1: (0, 0, 0)}
        xt_free2 = xt_free_t
        sc_free = {b: 0 for b in range(6)}
        sc_free_v = {b: 0 for b in range(6)}
        ptev_bank = {b: 0 for b in range(6)}
        ptsl_free = {0: 0, 1: 0}
        psl_free = {0: 0, 1: 0}
        zpv_tag = 0
        sq_free_s = 0
        XB = lambda sp: (xbuf if sp % 2 == 0 else wbuf[:, 8192:16384].bitcast(F32))
        xloads = {}

        def x_load(sp):
            if sp >= NSUP:
                return
            fr = xb_free[sp % 2]
            wait("sync", sv, fr[0])
            wait("sync", st_, fr[1])
            xloads[sp] = qld.inc("sync", lambda e, sp=sp, x_sup=x_sup: e.dma_start(
                out=XB(sp).rearrange("p (q d) -> p q d", q=4), in_=x_sup[sp]))

        x_load(0)
        for sup in range(NSUP):
            x_load(sup + 1)
            xb = XB(sup)
            wait_t("tensor", xloads[sup])
            wait("tensor", st_, xt_free2)
            ev_tag = {0: 0, 1: 0}
            for dc in range(8):
                bk = dc % 2
                wait("tensor", ss, ev_tag[bk])
                for tt2 in range(4):
                    tx_f32(xb[:, tt2 * 1024 + dc * 128: tt2 * 1024 + dc * 128 + 128],
                           tt2, bk)
                ttag = mark("tensor", st_)
                wait("scalar", st_, ttag)
                ev_tag[bk] = inc("scalar", ss, lambda e, dc=dc, bk=bk: e.activation(
                    XT(dc), XPSB[bk][:], ACT.Copy))
            wait("tensor", ss, ev_tag[1])
            xb_free_t = ttag
            wait("gpsimd", st_, xb_free_t)
            wait_t("gpsimd", xloads[sup])
            bvtag = qgl.inc("gpsimd", lambda e, xb=xb: e.dma_start(
                out=xb.rearrange("p (q d) -> p q d", q=4),
                in_=bv_d[None, None, :].to_broadcast([128, 4, D]),
                accum_op=ALU.add))
            for kc in range(8):
                b = kc % 4
                wait("tensor", ss, kv_free[b])
                for dc in range(8):
                    run("tensor", lambda e, dc=dc, kc=kc, b=b: nc.tensor.matmul(
                        s_ps[:, b * 512:(b + 1) * 512],
                        WAT(0, dc)[:, kc * 128:(kc + 1) * 128],
                        XT(dc), start=(dc == 0), stop=(dc == 7)))
                qtag = mark("tensor", st_)
                wait("scalar", st_, qtag)
                kv_free[b] = inc("scalar", ss, lambda e, kc=kc, b=b: e.activation(
                    QT(kc), s_ps[:, b * 512:(b + 1) * 512],
                    ACT.Identity, bias=stat[:, BQC + kc:BQC + 1 + kc]))
            q_done_s = ss.n

            for sub in range(4):
                i = sup * 4 + sub
                nfull = i // 4
                dw = (i % 4 + 1) * 128
                chunks = [(c * 512, 512, False) for c in range(nfull)]
                chunks.append((nfull * 512, dw, True))
                nch = len(chunks)
                waves = [chunks] if nch <= 6 else [chunks[:6], chunks[6:]]
                wait("tensor", ss, q_done_s)
                first_pv = True
                lcols = 0
                last_exp = 0
                for wv_i, wave in enumerate(waves):
                    for ci, (off, w, isdiag) in enumerate(wave):
                        b = ci
                        wait("tensor", ss, sc_free[b])
                        wait("tensor", sv, sc_free_v[b])
                        wait("tensor", sv, ptev_bank[b])
                        for kc in range(8):
                            run("tensor", lambda e, kc=kc, off=off, w=w, b=b, sub=sub:
                                nc.tensor.matmul(
                                    s_ps[:, b * 512: b * 512 + w],
                                    QT(kc)[:, sub * 128:(sub + 1) * 128],
                                    KT(kc)[:, off:off + w],
                                    start=(kc == 0), stop=(kc == 7)))
                        stag = mark("tensor", st_)
                        score_mark[0] = stag
                        wait("vector", st_, stag)
                        if isdiag:
                            run("vector", lambda e, b=b, w=w: e.tensor_tensor(
                                s_ps[:, b * 512 + w - 128: b * 512 + w],
                                s_ps[:, b * 512 + w - 128: b * 512 + w],
                                mask_sb[:], ALU.add))
                            vsync()
                        sc_free_v[b] = inc("vector", sv, lambda e, b=b, w=w, ci=ci:
                                           e.reduce_max(stc(CMAXC + ci),
                                                        s_ps[:, b * 512: b * 512 + w], AX))
                    nw = len(wave)
                    vsync()
                    if wv_i == 0:
                        run("vector", lambda e, nw=nw: e.reduce_max(
                            stc(MISC + 0), stat[:, CMAXC:CMAXC + nw], AX))
                    else:
                        wait("vector", ss, last_exp)
                        run("vector", lambda e, nw=nw: e.reduce_max(
                            stc(MISC + 2), stat[:, CMAXC:CMAXC + nw], AX))
                        vsync()
                        run("vector", lambda e: e.tensor_tensor(
                            stc(MISC + 3), stc(MISC + 0), stc(MISC + 2), ALU.max))
                        vsync()
                        run("vector", lambda e: e.tensor_tensor(
                            stc(MISC + 4), stc(MISC + 0), stc(MISC + 3), ALU.subtract))
                        vsync()
                        run("vector", lambda e: e.tensor_scalar(
                            stc(MISC + 4), stc(MISC + 4), 1.0 / 32.0, 0.0, ALU.mult, ALU.add))
                        run("vector", lambda e: e.tensor_copy(stc(MISC + 0), stc(MISC + 3)))
                    vsync()
                    run("vector", lambda e: e.tensor_scalar(
                        stc(MISC + 1), stc(MISC + 0), -1.0 / 32.0, 0.0, ALU.mult, ALU.add))
                    if wv_i == 1:
                        vt_ = mark("vector", sv)
                        wait("scalar", sv, vt_)
                        st2 = inc("scalar", ss, lambda e: e.activation(
                            stc(MISC + 4), stc(MISC + 4), ACT.Exp))
                        wait("vector", ss, st2)
                        wait("vector", st_, zpv_tag)
                        run("vector", lambda e: e.tensor_scalar(
                            HT_, z_ps[:], stc(MISC + 4), 0.0, ALU.mult, ALU.add))
                        run("vector", lambda e, lcols=lcols: e.tensor_scalar(
                            stat[:, LPARTC:LPARTC + lcols], stat[:, LPARTC:LPARTC + lcols],
                            stc(MISC + 4), 0.0, ALU.mult, ALU.add))
                        zres = mark("vector", sv)
                        wait("tensor", sv, zres)
                        first_pv = True
                    mv_tag = zres if wv_i == 1 else mark("vector", sv)
                    wait("scalar", sv, mv_tag)
                    for ci, (off, w, isdiag) in enumerate(wave):
                        b = ci
                        ns = w // 128
                        pslot = ci % 2
                        wait("scalar", st_, psl_free[pslot])
                        sc_free[b] = inc("scalar", ss, lambda e, b=b, w=w, pslot=pslot,
                                         lci=lcols: e.activation(
                                             PSL(pslot)[:, 0:w], s_ps[:, b * 512: b * 512 + w],
                                             ACT.Exp, bias=stc(MISC + 1), scale=1.0 / 32.0,
                                             accum_out=stc(LPARTC + lci)))
                        last_exp = sc_free[b]
                        lcols += 1
                        wait("tensor", ss, sc_free[b])
                        pbank = s_ps.bitcast(BF16)[:, b * 1024:(b + 1) * 1024]
                        for j in range(ns):
                            run("tensor", lambda e, a=PSL(pslot)[:, j * 128:(j + 1) * 128],
                                j=j, pbank=pbank: nc.tensor.transpose(
                                    pbank[:, j * 128:(j + 1) * 128], a, id_sb[:]))
                        psl_free[pslot] = mark("tensor", st_)
                        ptslot = ci % 2
                        wait("vector", st_, psl_free[pslot])
                        wait("vector", st_, ptsl_free[ptslot])
                        ptev = inc("vector", sv, lambda e, ns=ns, ptslot=ptslot, pbank=pbank:
                                   e.tensor_copy(PTSL(ptslot)[:, 0:ns * 128],
                                                 pbank[:, 0:ns * 128]))
                        ptev_bank[b] = ptev
                        wait("tensor", sv, ptev)
                        wait("tensor", ss, sq_free_s)
                        last_chunk = (ci == len(wave) - 1)
                        for j in range(ns):
                            s_tile = off // 128 + j
                            lastj = last_chunk and (j == ns - 1)
                            for half in range(2):
                                run("tensor", lambda e, j=j, half=half, s_tile=s_tile,
                                    ptslot=ptslot, fp=first_pv, lastj=lastj: nc.tensor.matmul(
                                        z_ps[:, half * 512:(half + 1) * 512],
                                        PTSL(ptslot)[:, j * 128:(j + 1) * 128],
                                        Vt(s_tile)[:, half * 512:(half + 1) * 512],
                                        start=fp, stop=lastj))
                            first_pv = False
                        ptsl_free[ptslot] = mark("tensor", st_)
                    zpv_tag = st_.n
                wait("vector", st_, zpv_tag)
                wait("vector", ss, last_exp)
                wait_t("vector", bvtag)
                if len(waves) == 2:
                    run("vector", lambda e: e.scalar_tensor_tensor(
                        HT_, z_ps[:], 1.0, HT_, ALU.mult, ALU.add))
                    vsync()
                run("vector", lambda e, lcols=lcols: e.reduce_sum(
                    stc(MISC + 5), stat[:, LPARTC:LPARTC + lcols], AX))
                vsync()
                run("vector", lambda e: e.reciprocal(stc(MISC + 6), stc(MISC + 5)))
                vsync()
                zslot = i % 2
                wait_t("vector", zsl_free[zslot][0])
                wait("vector", ss, zsl_free[zslot][1])
                zsrc = HT_ if len(waves) == 2 else z_ps[:]
                xbt = xb[:, sub * 1024:(sub + 1) * 1024]
                ztag = inc("vector", sv, lambda e, zslot=zslot, zsrc=zsrc, xbt=xbt:
                           e.scalar_tensor_tensor(
                               ZSL(zslot), zsrc, stc(MISC + 6), xbt,
                               ALU.mult, ALU.add, accum_out=stc(MISC + 7)))
                wait("scalar", sv, ztag)
                sqtag = inc("scalar", ss, lambda e, zslot=zslot: e.activation(
                    z_ps[:], ZSL(zslot), ACT.Square, accum_out=stc(MISC + 8)))
                sq_free_s = sqtag
                wait("vector", sv, ztag)
                wait("vector", ss, sqtag)
                run("vector", lambda e, i=i: e.tensor_scalar(
                    stc(M1C + i), stc(MISC + 7), 1.0 / D, 0.0, ALU.mult, ALU.add))
                vsync()
                run("vector", lambda e, i=i: e.tensor_tensor(
                    stc(MISC + 9), stc(M1C + i), stc(M1C + i), ALU.mult))
                run("vector", lambda e: e.tensor_scalar(
                    stc(MISC + 8), stc(MISC + 8), 1.0 / D, 0.0, ALU.mult, ALU.add))
                vsync()
                vstat = inc("vector", sv, lambda e, i=i: e.tensor_tensor(
                    stc(V1C + i), stc(MISC + 8), stc(MISC + 9), ALU.subtract))
                wait("scalar", sv, ztag)
                stotag = qso.inc("scalar", lambda e, i=i, zslot=zslot: e.dma_start(
                    out=z_sd[i], in_=ZSL(zslot)))
                zsl_free[zslot] = (stotag, sqtag, ztag)
            xb_free[sup % 2] = (sv.n, xb_free_t, 0)
            xt_free2 = st_.n
        attn_v = sv.n

        if debug and sq == 0:
            wait_ts("scalar", qso.bar())
            for tt in range(NT):
                qso.inc("scalar", lambda e, tt=tt: e.dma_start(
                    out=dbg["z"][tt], in_=z_sd[tt]))

        if stage < 4:
            break
        wait("sync", st_, score_mark[0])
        gu_done = []
        for dc in range(8):
            gu_done.append(qld.inc("sync", lambda e, dc=dc: e.dma_start(
                out=WGT(dc), in_=wguT_s[0, dc])))
        wait("sync", st_, st_.n)
        pre_zld = {}
        for t in range(2):
            wait_t("sync", zsl_free[t][0])
            wait("sync", ss, zsl_free[t][1])
            pre_zld[t] = qld.inc("sync", lambda e, t=t: e.dma_start(
                out=ZSL(t), in_=z_sd[t]))
        for dc in range(8):
            gu_done.append(qld.inc("sync", lambda e, dc=dc: e.dma_start(
                out=WUT(dc), in_=wguT_s[1, dc])))
        wait("vector", sv, attn_v)
        run("vector", lambda e: e.tensor_scalar(
            stat[:, MISC + 0:MISC + 0 + NT], stat[:, V1C:V1C + NT], EPS, 1.0,
            ALU.add, ALU.mult))
        vtag = mark("vector", sv)
        wait("scalar", sv, vtag)
        stag = inc("scalar", ss, lambda e: e.activation(
            stat[:, R1C:R1C + NT], stat[:, MISC + 0:MISC + 0 + NT], ACT.Sqrt))
        wait("vector", ss, stag)
        r1tag = inc("vector", sv, lambda e: e.reciprocal(
            stat[:, R1C:R1C + NT], stat[:, R1C:R1C + NT]))
        wait("vector", sv, r1tag)

        zld_free = {0: (0, 0), 1: (0, 0)}
        PSLOT = [s_ps[:, b * 512:(b + 1) * 512] for b in range(5)] + \
                [z_ps[:, 0:512], z_ps[:, 512:1024]]
        gu_pool = [0] * 7
        gu_pool_s = [0] * 7
        t1_free = {0: 0, 1: 0}
        prev_pev = 0
        MP = lambda t: MISC + 24 * (t % 2)
        PINTB = lambda t: (PINT if t % 2 == 0 else qt[:, 0:4096])
        PSTGB = lambda t: (PSTG if t % 2 == 0 else wbuf[:, 0:4096])
        gu_last_mm = [0]
        chain_tag = {}
        pq_tags = {}
        pass2_tag = {}
        hp2_tag = {}
        htev_tag = {}
        pstg_stores = []

        def gu_chain(t):
            zslot = t % 2
            mp = MP(t)
            if t < 2:
                zldt = pre_zld[t]
            else:
                wait("sync", sv, zld_free[zslot][0])
                wait("sync", ss, zld_free[zslot][1])
                zldt = qld.inc("sync", lambda e, t=t, zslot=zslot: e.dma_start(
                    out=ZSL(zslot), in_=z_sd[t]))
            wait_t("vector", zldt)
            run("vector", lambda e, zslot=zslot, t=t: e.tensor_scalar(
                ZSL(zslot), ZSL(zslot), stc(M1C + t), 1.0, ALU.subtract, ALU.mult))
            vsync()
            run("vector", lambda e, zslot=zslot, mp=mp: e.reduce_max(
                stc(mp + 0), ZSL(zslot), AX, apply_absolute_value=True))
            vsync()
            run("vector", lambda e, t=t, mp=mp: e.tensor_tensor(
                stc(mp + 1), stc(mp + 0), stc(R1C + t), ALU.mult))
            vsync()
            run("vector", lambda e, mp=mp: e.tensor_scalar(
                stc(mp + 1), stc(mp + 1), 1e-5, 1.0 / 127.0, ALU.max, ALU.mult))
            vsync()
            run("vector", lambda e, mp=mp: e.reciprocal(stc(mp + 2), stc(mp + 1)))
            vsync()
            run("vector", lambda e, t=t, mp=mp: e.tensor_tensor(
                stc(mp + 3), stc(mp + 2), stc(R1C + t), ALU.mult))
            run("vector", lambda e, mp=mp: e.tensor_tensor(
                stc(mp + 4), stc(MGC), stc(mp + 1), ALU.mult))
            vq = inc("vector", sv, lambda e, mp=mp: e.tensor_tensor(
                stc(mp + 5), stc(MGC + 1), stc(mp + 1), ALU.mult))
            chain_tag[t] = (vq, zslot)

        def gu_hint(t):
            nonlocal prev_pev
            vq, zslot = chain_tag[t]
            mp = MP(t)
            wait("scalar", sv, vq)
            wait("scalar", sv, gu_pool[5])
            wait("scalar", sv, gu_pool[6])
            hp1 = inc("scalar", ss, lambda e, zslot=zslot, mp=mp: e.activation(
                z_ps[:], ZSL(zslot), ACT.Identity, bias=stc(MAGICC), scale=stc(mp + 3)))
            wait("vector", ss, hp1)
            hp2 = inc("vector", sv, lambda e: e.tensor_scalar(
                HINT, z_ps[:], -MAGIC, 1.0, ALU.add, ALU.mult))
            hp2_tag[t] = hp2
            zld_free[zslot] = (hp2, hp1)
            wait("tensor", sv, hp2)
            wait("tensor", sv, prev_pev)
            for dc in range(8):
                tx_bf16(HINT[:, dc * 128:(dc + 1) * 128], dc)
            htx = mark("tensor", st_)
            wait("vector", st_, htx)
            htev_tag[t] = inc("vector", sv, lambda e: e.tensor_copy(
                wbuf[:, 9216:10240], pt_ps[:, 0:1024]))

        def gu_chunks(t):
            nonlocal prev_pev
            mp = MP(t)
            wait("tensor", sv, htev_tag[t])
            prev = pq_tags.get(t - 1)

            for c in range(8):
                sg = (2 * c) % 7
                su = (2 * c + 1) % 7
                wait("tensor", sv, gu_pool[sg])
                wait("tensor", ss, gu_pool_s[sg])
                wait("tensor", sv, gu_pool[su])
                wait("tensor", ss, gu_pool_s[su])
                for dc in range(8):
                    wait_ts("tensor", [gu_done[dc]])
                    run("tensor", lambda e, dc=dc, c=c, sg=sg: nc.tensor.matmul(
                        PSLOT[sg], HTT(dc), WGT(dc)[:, c * 512:(c + 1) * 512],
                        start=(dc == 0), stop=(dc == 7)))
                    wait_ts("tensor", [gu_done[8 + dc]])
                    run("tensor", lambda e, dc=dc, c=c, su=su: nc.tensor.matmul(
                        PSLOT[su], HTT(dc), WUT(dc)[:, c * 512:(c + 1) * 512],
                        start=(dc == 0), stop=(dc == 7)))
                gutag = mark("tensor", st_)
                gu_last_mm[0] = gutag
                t1s = c % 2
                wait("scalar", st_, gutag)
                wait("scalar", sv, t1_free[t1s])
                gu_pool_s[sg] = inc("scalar", ss, lambda e, sg=sg, t1s=t1s, mp=mp:
                                    e.activation(T1(t1s), PSLOT[sg], ACT.Silu,
                                                 scale=stc(mp + 4)))
                wait("vector", ss, gu_pool_s[sg])
                if prev is not None:
                    wait("vector", ss, prev[c // 2])
                ptag = inc("vector", sv, lambda e, su=su, c=c, t1s=t1s: e.tensor_tensor(
                    PBUF[:, c * 512:(c + 1) * 512], T1(t1s), PSLOT[su], ALU.mult))
                gu_pool[su] = ptag
                gu_pool[sg] = ptag
                gu_pool_s[su] = gu_pool_s[sg]
                t1_free[t1s] = ptag
                wait("vector", sv, ptag)
                run("vector", lambda e, c=c, mp=mp: e.reduce_max(
                    stc(mp + 8 + c), PBUF[:, c * 512:(c + 1) * 512], AX,
                    apply_absolute_value=True))

        def gu_finalize(t):
            mp = MP(t)
            vsync()
            run("vector", lambda e, mp=mp: e.reduce_max(
                stc(mp + 16), stat[:, mp + 8:mp + 16], AX))
            vsync()
            run("vector", lambda e, mp=mp: e.tensor_tensor(
                stc(mp + 17), stc(mp + 16), stc(mp + 5), ALU.mult))
            vsync()
            run("vector", lambda e, mp=mp: e.tensor_scalar(
                stc(mp + 17), stc(mp + 17), 1e-5, 1.0 / 127.0, ALU.max, ALU.mult))
            vsync()
            run("vector", lambda e, mp=mp: e.reciprocal(stc(mp + 18), stc(mp + 17)))
            vsync()
            run("vector", lambda e, mp=mp: e.tensor_tensor(
                stc(mp + 19), stc(mp + 18), stc(mp + 5), ALU.mult))
            vq2 = inc("vector", sv, lambda e, t=t, mp=mp: e.tensor_tensor(
                stc(GAMC + t), stc(mp + 17), stc(MGC + 2), ALU.mult))
            wait("scalar", sv, vq2)
            if t + 1 in hp2_tag:
                wait("scalar", sv, hp2_tag[t + 1])
            tags = []
            pv_last = 0
            for piece in range(4):
                wait("scalar", sv, pv_last)
                pp1 = inc("scalar", ss, lambda e, piece=piece, mp=mp: e.activation(
                    z_ps[:], PBUF[:, piece * 1024:(piece + 1) * 1024], ACT.Identity,
                    bias=stc(MAGICC), scale=stc(mp + 19)))
                tags.append(pp1)
                wait("vector", ss, pp1)
                pv_last = inc("vector", sv, lambda e, piece=piece, t=t: e.tensor_scalar(
                    PINTB(t)[:, piece * 1024:(piece + 1) * 1024], z_ps[:], -MAGIC, 1.0,
                    ALU.add, ALU.mult))
            pq_tags[t] = tags
            pass2_tag[t] = pv_last
            gu_pool[5] = pv_last
            gu_pool[6] = pv_last
            gu_pool_s[5] = gu_pool_s[6] = ss.n

        def gu_ptx(t):
            nonlocal prev_pev
            if t < 0:
                return
            wait("tensor", sv, pass2_tag[t])
            pstg_last = pstg_stores[-2] if len(pstg_stores) >= 2 else None
            for ic0 in range(0, 32, 8):
                for cc in range(8):
                    tx_bf16(PINTB(t)[:, (ic0 + cc) * 128:(ic0 + cc + 1) * 128], cc)
                ptx = mark("tensor", st_)
                wait("vector", st_, ptx)
                wait_t("vector", pstg_last)
                prev_pev = inc("vector", sv, lambda e, ic0=ic0, t=t: e.tensor_copy(
                    PSTGB(t)[:, ic0 * 128:(ic0 + 8) * 128],
                    pt_ps[:, 0:1024]))
                wait("tensor", sv, prev_pev)
            wait("scalar", sv, prev_pev)
            pstg_stores.append(qso.inc("scalar", lambda e, t=t: e.dma_start(
                out=pint_s[t], in_=PSTGB(t))))

        gu_chain(0)
        gu_hint(0)
        for tt in range(NT):
            if tt + 1 < NT:
                gu_chain(tt + 1)
            gu_chunks(tt)
            gu_ptx(tt - 1)
            if tt + 1 < NT:
                gu_hint(tt + 1)
            gu_finalize(tt)
        gu_ptx(NT - 1)

        if debug and sq == 0:
            wait_ts("scalar", qso.bar())
            for tt in range(NT):
                qso.inc("scalar", lambda e, tt=tt: e.dma_start(
                    out=dbg["pint"][tt], in_=pint_s[tt]))

        if stage < 5:
            break
        wait_ts("sync", pstg_stores[-2:])
        pld0_t = qld.inc("sync", lambda e: e.dma_start(out=PTLD(0), in_=pint_s[0]))
        if sq + 1 < SEQS:
            wait("sync", ss, pq_tags[NT - 1][-1])
            x0_pre[sq + 1] = qld.inc("sync", lambda e, sq=sq: e.dma_start(
                out=xbuf.rearrange("p (q d) -> p q d", q=4),
                in_=x_d[sq + 1].rearrange("(u q p) d -> u p q d", p=128, q=4)[0]))
        wait("sync", st_, gu_last_mm[0])
        wdld_h1 = []
        wdld = []
        for ic0 in range(0, 32, 8):
            tk = qld.inc("sync", lambda e, ic0=ic0: e.dma_start(
                out=arena[:, ic0 * 1024:(ic0 + 8) * 1024], in_=wdT_s[ic0:ic0 + 8]))
            (wdld_h1 if ic0 < 16 else wdld).append(tk)
        wait_ts("tensor", wdld_h1[:1])
        ptld_free = {0: 0, 1: 0}
        yfree = {0: 0, 1: 0}
        ofree = {0: None, 1: None}
        zld2_free = {0: (0, 0), 1: (0, 0)}
        for tt in range(NT):
            b = tt % 2
            if tt == 0:
                pldt = pld0_t
            else:
                wait("sync", st_, ptld_free[b])
                pldt = qld.inc("sync", lambda e, tt=tt, b=b: e.dma_start(
                    out=PTLD(b), in_=pint_s[tt]))
            zslot = tt % 2
            wait("sync", sv, zld2_free[zslot][0])
            wait("sync", ss, zld2_free[zslot][1])
            zldt = qld.inc("sync", lambda e, tt=tt, zslot=zslot: e.dma_start(
                out=ZSL(zslot), in_=z_sd[tt]))
            wait_t("tensor", pldt)
            wait("tensor", sv, yfree[b])
            for ic in range(32):
                if ic == 8:
                    wait_ts("tensor", wdld_h1[1:])
                elif ic == 16:
                    wait_ts("tensor", wdld[:1])
                elif ic == 24:
                    wait_ts("tensor", wdld[1:])
                for half in range(2):
                    run("tensor", lambda e, ic=ic, half=half, b=b: nc.tensor.matmul(
                        s_ps[:, b * 1024 + half * 512: b * 1024 + (half + 1) * 512],
                        PTLD(b)[:, ic * 128:(ic + 1) * 128],
                        WDT(ic)[:, half * 512:(half + 1) * 512],
                        start=(ic == 0), stop=(ic == 31)))
            ytag = mark("tensor", st_)
            wait_t("vector", zldt)
            httag = inc("vector", sv, lambda e, zslot=zslot, tt=tt: e.tensor_scalar(
                HT_, ZSL(zslot), stc(M1C + tt), stc(R1C + tt), ALU.subtract, ALU.mult))
            ob = tt % 2
            wait_t("vector", ofree[ob])
            wait("vector", st_, ytag)
            wait("vector", sv, httag)
            z2tag = inc("vector", sv, lambda e, b=b, tt=tt, ob=ob: e.scalar_tensor_tensor(
                OBUF(ob), s_ps[:, b * 1024:(b + 1) * 1024], stc(GAMC + tt), HT_,
                ALU.mult, ALU.add, accum_out=stc(MISC + 0)))
            zld2_free[zslot] = (httag, 0)
            yfree[b] = z2tag
            wait("scalar", sv, z2tag)
            sq2 = inc("scalar", ss, lambda e, ob=ob: e.activation(
                z_ps[:], OBUF(ob), ACT.Square, accum_out=stc(MISC + 1)))
            wait("vector", sv, z2tag)
            wait("vector", ss, sq2)
            run("vector", lambda e: e.tensor_scalar(
                stc(MISC + 2), stc(MISC + 0), 1.0 / D, 0.0, ALU.mult, ALU.add))
            vsync()
            run("vector", lambda e: e.tensor_tensor(
                stc(MISC + 3), stc(MISC + 2), stc(MISC + 2), ALU.mult))
            run("vector", lambda e: e.tensor_scalar(
                stc(MISC + 1), stc(MISC + 1), 1.0 / D, EPS, ALU.mult, ALU.add))
            vsync()
            v2tag = inc("vector", sv, lambda e: e.tensor_tensor(
                stc(MISC + 4), stc(MISC + 1), stc(MISC + 3), ALU.subtract))
            wait("scalar", sv, v2tag)
            sqr2 = inc("scalar", ss, lambda e: e.activation(
                stc(MISC + 5), stc(MISC + 4), ACT.Sqrt))
            wait("vector", ss, sqr2)
            run("vector", lambda e: e.reciprocal(stc(MISC + 6), stc(MISC + 5)))
            vsync()
            otag = inc("vector", sv, lambda e, ob=ob: e.tensor_scalar(
                OBUF(ob), OBUF(ob), stc(MISC + 2), stc(MISC + 6), ALU.subtract, ALU.mult))
            wait("scalar", sv, otag)
            ofree[ob] = qso.inc("scalar", lambda e, tt=tt, ob=ob, out_seq=out_seq:
                                e.dma_start(out=out_seq[tt], in_=OBUF(ob)))
            ptld_free[b] = st_.n

    wait_ts("sync", qso.bar())
    drain_queues()
    return _finish(nc, ctx, block_emit)


def _consts():
    ident = np.eye(128, dtype=ml_dtypes.bfloat16)
    mask = np.zeros((128, 128), dtype=np.float32)
    for p in range(128):
        mask[p, p + 1:] = NEG
    return ident, mask


def kernel(**inputs):
    x = np.asarray(inputs["L0_attn_x"], dtype=np.float32)
    ident, mask = _consts()
    base = dict(
        wq=np.asarray(inputs["L0_attn_Wq"], dtype=np.float32),
        wk=np.asarray(inputs["L0_attn_Wk"], dtype=np.float32),
        wv=np.asarray(inputs["L0_attn_Wv"], dtype=np.float32),
        bq=np.asarray(inputs["L0_attn_bq"], dtype=np.float32),
        bk=np.asarray(inputs["L0_attn_bk"], dtype=np.float32),
        bv=np.asarray(inputs["L0_attn_bv"], dtype=np.float32),
        wg=np.asarray(inputs["L0_ffn_w_gate"], dtype=np.float32),
        wu=np.asarray(inputs["L0_ffn_w_up"], dtype=np.float32),
        wd=np.asarray(inputs["L0_ffn_w_down"], dtype=np.float32),
        ident=ident, identf=np.eye(128, dtype=np.float32), mask=mask,
        magicv=np.full((128, 1), MAGIC, dtype=np.float32),
    )
    nc = build_program(debug=False)
    in_maps = []
    for c in range(8):
        m = dict(base)
        m["x"] = np.ascontiguousarray(x[2 * c:2 * c + 2])
        in_maps.append(m)
    res = run_bass_kernel_spmd(nc, in_maps, core_ids=list(range(8)))
    out = np.concatenate([np.asarray(r["out"]) for r in res.results], axis=0)
    return out.reshape(16, 4096, 1024).astype(np.float32, copy=False)


# version 18
# speedup vs baseline: 1.1870x; 1.1870x over previous
import numpy as np
import ml_dtypes
import concourse.bass as bass
import concourse.mybir as mybir
from concourse.bass_utils import run_bass_kernel_spmd
from contextlib import ExitStack

F32 = mybir.dt.float32
BF16 = mybir.dt.bfloat16
ALU = mybir.AluOpType
ACT = mybir.ActivationFunctionType
AX = mybir.AxisListType.X

MAGIC = float(np.float32(1.5 * 2.0 ** 23))
NEG = -30000.0
SEQS = 2
S = 4096
D = 1024
INNER = 4096
NT = S // 128
NSUP = S // 512
EPS = 1e-5

ENGS = ("sync", "gpsimd", "tensor", "vector", "scalar")


def _finish(nc, ctx, block_emit):
    block_emit()
    ctx.close()
    return nc


def build_program(debug=False, stage=6):
    nc = bass.Bass("TRN2", target_bir_lowering=False, debug=False,
                   detect_race_conditions=False)

    x_d = nc.dram_tensor("x", [SEQS, S, D], F32, kind="ExternalInput").ap()
    wq_d = nc.dram_tensor("wq", [D, D], F32, kind="ExternalInput").ap()
    wk_d = nc.dram_tensor("wk", [D, D], F32, kind="ExternalInput").ap()
    wv_d = nc.dram_tensor("wv", [D, D], F32, kind="ExternalInput").ap()
    bq_d = nc.dram_tensor("bq", [D], F32, kind="ExternalInput").ap()
    bk_d = nc.dram_tensor("bk", [D], F32, kind="ExternalInput").ap()
    bv_d = nc.dram_tensor("bv", [D], F32, kind="ExternalInput").ap()
    wg_d = nc.dram_tensor("wg", [INNER, D], F32, kind="ExternalInput").ap()
    wu_d = nc.dram_tensor("wu", [INNER, D], F32, kind="ExternalInput").ap()
    wd_d = nc.dram_tensor("wd", [D, INNER], F32, kind="ExternalInput").ap()
    id_d = nc.dram_tensor("ident", [128, 128], BF16, kind="ExternalInput").ap()
    idf_d = nc.dram_tensor("identf", [128, 128], F32, kind="ExternalInput").ap()
    mask_d = nc.dram_tensor("mask", [128, 128], F32, kind="ExternalInput").ap()
    magic_d = nc.dram_tensor("magicv", [128, 1], F32, kind="ExternalInput").ap()
    out_d = nc.dram_tensor("out", [SEQS, S, D], F32, kind="ExternalOutput").ap()

    wqkvT_s = nc.dram_tensor("wqkvT_s", [3, 8, 128, D], BF16, kind="Internal").ap()
    wguT_s = nc.dram_tensor("wguT_s", [2, 8, 128, INNER], BF16, kind="Internal").ap()
    wdT_s = nc.dram_tensor("wdT_s", [32, 128, D], BF16, kind="Internal").ap()
    z_sd = nc.dram_tensor("z_s", [NT, 128, D], F32, kind="Internal").ap()
    pint_s = nc.dram_tensor("pint_s", [NT, 128, INNER], BF16, kind="Internal").ap()
    mw_s = nc.dram_tensor("mw_s", [4], F32, kind="Internal").ap()

    dbg = {}
    if debug:
        dbg["kt"] = nc.dram_tensor("dbg_kt", [8, 128, S], BF16, kind="ExternalOutput").ap()
        dbg["v"] = nc.dram_tensor("dbg_v", [NT, 128, D], BF16, kind="ExternalOutput").ap()
        dbg["z"] = nc.dram_tensor("dbg_z", [NT, 128, D], F32, kind="ExternalOutput").ap()
        dbg["wgT"] = nc.dram_tensor("dbg_wgT", [8, 128, INNER], BF16, kind="ExternalOutput").ap()
        dbg["wdT"] = nc.dram_tensor("dbg_wdT", [32, 128, D], BF16, kind="ExternalOutput").ap()
        dbg["mw"] = nc.dram_tensor("dbg_mw", [4], F32, kind="ExternalOutput").ap()
        dbg["pint"] = nc.dram_tensor("dbg_pint", [NT, 128, INNER], BF16, kind="ExternalOutput").ap()

    ops = {e: [] for e in ENGS}
    waited = {e: {} for e in ENGS}

    def emit(eng, fn):
        ops[eng].append(fn)

    class Sem:
        __slots__ = ("name", "n", "h")

        def __init__(self, name):
            self.name, self.n, self.h = name, 0, None

    st_ = Sem("st")
    sv = Sem("sv")
    ss = Sem("ss")
    gsem = Sem("gs")
    allsems = [st_, sv, ss, gsem]

    class DmaQ:

        def __init__(self, name, k):
            self.sems = [Sem(f"{name}{i}") for i in range(k)]
            allsems.extend(self.sems)
            self.i = 0

        def inc(self, eng, fn):
            s_ = self.sems[self.i % len(self.sems)]
            self.i += 1
            wait(eng, s_, s_.n)
            s_.n += 16
            g = lambda e, fn=fn, s_=s_: fn(e).then_inc(s_.h, 16)
            g._has_inc = True
            emit(eng, g)
            return (s_, s_.n)

        def bar(self):
            return [(s_, s_.n) for s_ in self.sems if s_.n > 0]

    def wait(eng, s_, val):
        if val <= 0:
            return
        if waited[eng].get(s_.name, 0) >= val:
            return
        waited[eng][s_.name] = val
        g = lambda e, s_=s_, val=val: e.wait_ge(s_.h, val)
        g._is_wait = True
        emit(eng, g)

    def inc(eng, s_, fn, n=1):
        s_.n += n
        g = lambda e, fn=fn, s_=s_, n=n: fn(e).then_inc(s_.h, n)
        g._has_inc = True
        emit(eng, g)
        return s_.n

    def run(eng, fn):
        emit(eng, fn)

    def vsync():
        f = ops["vector"][-1]
        assert not getattr(f, "_is_wait", False), "vsync after wait"
        if not getattr(f, "_has_inc", False):
            mark("vector", sv)
        wait("vector", sv, sv.n)

    def wait_t(eng, tkt):
        if tkt:
            wait(eng, tkt[0], tkt[1])

    def wait_ts(eng, tkts):
        best = {}
        for t in tkts:
            if t and (t[0].name not in best or best[t[0].name][1] < t[1]):
                best[t[0].name] = t
        for t in best.values():
            wait(eng, t[0], t[1])

    qld = DmaQ("ld", 8)
    qgl = DmaQ("gl", 4)
    qso = DmaQ("so", 8)

    def drain_queues():
        wait_ts("sync", qld.bar())
        wait_ts("gpsimd", qgl.bar())
        wait_ts("scalar", qso.bar())

    def mark(eng, s_, n=1):
        s_.n += n
        f = ops[eng].pop()
        assert not getattr(f, "_has_inc", False), f"double inc on {eng}"
        g = lambda e, f=f, s_=s_, n=n: f(e).then_inc(s_.h, n)
        g._has_inc = True
        ops[eng].append(g)
        return s_.n

    ctx = ExitStack()
    sbuf = lambda name, shape, dt: ctx.enter_context(nc.sbuf_tensor(name, shape, dt))
    psum = lambda name, shape, dt: ctx.enter_context(nc.psum_tensor(name, shape, dt))

    arena = sbuf("arena", [128, 65536], BF16)
    wbuf = sbuf("wbuf", [128, 16384], BF16)
    xbuf = sbuf("xbuf", [128, 4096], F32)
    xt = sbuf("xt", [128, 4096], BF16)
    qt = sbuf("qt", [128, 4096], BF16)
    zbuf = sbuf("zbuf", [128, 2048], F32)
    htb = sbuf("htb", [128, 1024], F32)
    id_sb = sbuf("id_sb", [128, 128], BF16)
    idf_sb = sbuf("idf_sb", [128, 128], F32)
    mask_sb = sbuf("mask_sb", [128, 128], F32)
    stat = sbuf("stat", [128, 224], F32)
    M1C, V1C, R1C, GAMC, CMAXC, LPARTC, MISC = 0, 32, 64, 96, 128, 136, 144
    BQC, BKC = 192, 200
    MAGICC = 208

    s_ps = psum("s_ps", [128, 3072], F32)
    z_ps = psum("z_ps", [128, 1024], F32)
    pt_ps = s_ps.bitcast(BF16)[:, 5120:6144]

    for s2 in allsems:
        s2.h = ctx.enter_context(nc.semaphore(s2.name))

    KT = lambda kc: arena[:, kc * 4096:(kc + 1) * 4096]
    Vt = lambda vt: arena[:, 32768 + vt * 1024: 32768 + (vt + 1) * 1024]
    WGT = lambda dc: arena[:, dc * 4096:(dc + 1) * 4096]
    WUT = lambda dc: arena[:, 32768 + dc * 4096: 32768 + (dc + 1) * 4096]
    WDT = lambda ic: arena[:, ic * 1024:(ic + 1) * 1024]
    WAT = lambda slot, dc: wbuf[:, slot * 8192 + dc * 1024: slot * 8192 + (dc + 1) * 1024]
    PTLD = lambda b: wbuf[:, b * 4096:(b + 1) * 4096]
    HINT = wbuf[:, 8192:9216]
    HTT = lambda dc: wbuf[:, 9216 + dc * 128: 9216 + (dc + 1) * 128]
    PSTG = wbuf[:, 10240:14336]
    XSUP = lambda tt2: xbuf[:, tt2 * 1024:(tt2 + 1) * 1024]
    PBUF = xbuf[:, 0:4096]
    OBUF = lambda b: qt.bitcast(F32)[:, b * 1024:(b + 1) * 1024]
    XT = lambda dc: xt[:, dc * 512:(dc + 1) * 512]
    PSL = lambda b: xt[:, b * 512:(b + 1) * 512]
    PTSL = lambda b: xt[:, 1024 + b * 512: 1024 + (b + 1) * 512]
    PINT = xt[:, 0:4096]
    QT = lambda kc: qt[:, kc * 512:(kc + 1) * 512]
    QSTG = qt[:, 0:4096]
    ZSL = lambda b: zbuf[:, b * 1024:(b + 1) * 1024]
    T1 = lambda b: htb[:, b * 512:(b + 1) * 512]
    HT_ = htb[:, 0:1024]

    stc = lambda c, w=1: stat[:, c:c + w]

    def block_emit():
        block = ctx.enter_context(nc.Block())

        def mk(engname):
            def body(e):
                for fn in ops[engname]:
                    fn(e)
            return body
        block.sync(mk("sync"))
        block.gpsimd(mk("gpsimd"))
        block.tensor(mk("tensor"))
        block.vector(mk("vector"))
        block.scalar(mk("scalar"))

    def tx_bf16(src_ap, slot):
        run("tensor", lambda e, a=src_ap, s=slot: nc.tensor.transpose(
            pt_ps[:, s * 128:(s + 1) * 128], a, id_sb[:]))

    XPSB = (s_ps[:, 2048:2560], s_ps[:, 2560:3072])

    def tx_f32(src_ap, slot, bank=0):
        run("tensor", lambda e, a=src_ap, s=slot, bank=bank: nc.tensor.transpose(
            XPSB[bank][:, s * 128:(s + 1) * 128], a, idf_sb[:]))

    t_consts = [
        qld.inc("sync", lambda e: e.dma_start(out=id_sb[:], in_=id_d[:])),
        qld.inc("sync", lambda e: e.dma_start(out=idf_sb[:], in_=idf_d[:])),
        qld.inc("sync", lambda e: e.dma_start(out=mask_sb[:], in_=mask_d[:])),
        qld.inc("sync", lambda e: e.dma_start(out=stat[:, MAGICC:MAGICC + 1], in_=magic_d[:])),
    ]
    wait_ts("scalar", t_consts)
    bias_tkts = [
        qgl.inc("gpsimd", lambda e: e.dma_start(
            out=stat[:, BQC:BQC + 8], in_=bq_d.rearrange("(c p) -> p c", p=128),
            allow_slow_non_contiguous=True)),
        qgl.inc("gpsimd", lambda e: e.dma_start(
            out=stat[:, BKC:BKC + 8], in_=bk_d.rearrange("(c p) -> p c", p=128),
            allow_slow_non_contiguous=True)),
    ]
    wait_ts("tensor", t_consts)

    if stage == 11:
        ops_done = True
    qkv_txdone = {}
    for wi, w_src in enumerate((wq_d, wk_d, wv_d) if (stage >= 12 or stage < 11) else ()):
        w_r = w_src.rearrange("(t p) d -> t p d", p=128)
        prev_store = store_tkts if wi > 0 else []
        half_tags = {}
        for half in range(2):
            base = (half % 2) * 4096
            wait("gpsimd", st_, qkv_txdone.get((wi - 1, half), 0))
            tags = []
            for j in range(4):
                kt_i = half * 4 + j
                tags.append(qgl.inc("gpsimd", lambda e, kt_i=kt_i, j=j, w_r=w_r, base=base:
                                    e.dma_start(out=wbuf[:, base + j * 1024: base + (j + 1) * 1024],
                                                in_=w_r[kt_i])))
            half_tags[half] = tags
        for half in range(2):
            base = (half % 2) * 4096
            wait_ts("tensor", half_tags[half])
            wait_ts("tensor", prev_store)
            for dc in range(8):
                for j in range(4):
                    tx_bf16(wbuf[:, base + j * 1024 + dc * 128: base + j * 1024 + dc * 128 + 128], j)
                ttag = mark("tensor", st_)
                qkv_txdone[(wi, half)] = ttag
                wait("vector", st_, ttag)
                vtag = inc("vector", sv, lambda e, dc=dc, half=half: e.tensor_copy(
                    wbuf[:, 8192 + dc * 1024 + half * 512: 8192 + dc * 1024 + half * 512 + 512],
                    pt_ps[:, 0:512]))
                wait("tensor", sv, vtag)
        vdone = sv.n
        wait("scalar", sv, vdone)
        store_tkts = []
        for dc in range(8):
            store_tkts.append(qso.inc("scalar", lambda e, wi=wi, dc=dc: e.dma_start(
                out=wqkvT_s[wi, dc], in_=wbuf[:, 8192 + dc * 1024: 8192 + (dc + 1) * 1024])))
        wait_ts("gpsimd", store_tkts)

    ACCG, ACCU, ACCD, RED1, ONESC = 176, 177, 178, 179, 180
    run("vector", lambda e: e.memset(stc(ONESC), 1.0))

    def mean_pass(w_src, ntiles, width, acc_col):
        flat = w_src.rearrange("(t p) d -> p t d", p=128)
        nch = ntiles * width // 2048
        red_tags = {}
        prev_v = sv.n
        prev_s = ss.n
        slots = [xbuf[:, 0:2048], xbuf[:, 2048:4096], zbuf[:, 0:2048],
                 qt.bitcast(F32)[:, 0:2048]]
        qmap = [("sync", qld), ("scalar", qso), ("gpsimd", qgl), ("sync", qld)]

        def chunk_ap(k):
            if width == 1024:
                return flat[:, 2 * k:2 * k + 2, :]
            return flat[:, k // 2, (k % 2) * 2048:(k % 2) * 2048 + 2048]

        def load(k):
            eng, q = qmap[k % 4]
            wait(eng, sv, red_tags.get(k - 4, prev_v))
            wait(eng, ss, prev_s)
            return q.inc(eng, lambda e, k=k:
                         e.dma_start(out=slots[k % 4], in_=chunk_ap(k)))

        ld_tags = {j: load(j) for j in range(min(4, nch))}
        for k in range(nch):
            wait_t("vector", ld_tags[k])
            red_tags[k] = inc("vector", sv, lambda e, k=k: e.reduce_sum(
                stc(MISC + k), slots[k % 4], AX,
                apply_absolute_value=True))
            if k + 4 < nch:
                ld_tags[k + 4] = load(k + 4)
        vsync()
        inc("vector", sv, lambda e, acc_col=acc_col, nch=nch: e.reduce_sum(
            stc(acc_col), stat[:, MISC:MISC + nch], AX))
        return sv.n

    if stage >= 13 or stage < 11:
        mean_pass(wg_d, 32, 1024, ACCG)
        mean_pass(wu_d, 32, 1024, ACCU)
        mean_pass(wd_d, 8, 4096, ACCD)
    wait_ts("gpsimd", store_tkts)
    wait("gpsimd", st_, st_.n)
    wkv_pref = []
    for wi, slot in ((1, 0), (2, 1)):
        for dc in range(8):
            wkv_pref.append(qgl.inc("gpsimd", lambda e, wi=wi, slot=slot, dc=dc:
                            e.dma_start(out=WAT(slot, dc), in_=wqkvT_s[wi, dc])))
    wait("tensor", sv, sv.n)
    wait("tensor", gsem, gsem.n)
    if 11 <= stage < 14:
        drain_queues()
        return _finish(nc, ctx, block_emit)
    run("tensor", lambda e: nc.tensor.matmul(
        z_ps[0:3, 0:1], stat[:, ACCG:ACCG + 3], stc(ONESC), start=True, stop=True))
    ttag = mark("tensor", st_)
    wait("scalar", st_, ttag)
    run("scalar", lambda e: e.activation(stat[0:3, RED1:RED1 + 1], z_ps[0:3, 0:1], ACT.Copy))
    stag = mark("scalar", ss)
    wait("scalar", ss, stag)
    mwtag = qso.inc("scalar", lambda e: e.dma_start(
        out=mw_s[0:3], in_=stat[0:3, RED1:RED1 + 1]))
    wait_t("sync", mwtag)
    MGC = 209
    SGC = 212
    mwld = qld.inc("sync", lambda e: e.dma_start(
        out=stat[:, MGC:MGC + 3], in_=mw_s[None, 0:3].to_broadcast([128, 3])))
    wait_t("vector", mwld)
    run("vector", lambda e: e.tensor_scalar(
        stat[:, MGC:MGC + 3], stat[:, MGC:MGC + 3], 1.0 / (INNER * D), 1e-5,
        ALU.mult, ALU.max))
    vsync()
    run("vector", lambda e: e.reciprocal(stat[:, SGC:SGC + 3], stat[:, MGC:MGC + 3]))
    vtag_sw = mark("vector", sv)
    wait("scalar", sv, vtag_sw)
    wait("tensor", sv, vtag_sw)

    def quant_pass(w_src, ntiles, width, s_col, arena_off, chunks_are_dc):
        flat = w_src.rearrange("(t p) d -> p t d", p=128)
        nch = ntiles * width // 2048
        arena_v = (arena[:, arena_off:arena_off + 32768]
                   .rearrange("p (dc c) -> p dc c", dc=8 if chunks_are_dc else 32))
        red_tags = {}
        qv_tags = {}
        prev_v = sv.n
        prev_s = ss.n
        stg_free = {0: 0, 1: 0}

        def chunk_ap(k):
            if width == 1024:
                return flat[:, 2 * k:2 * k + 2, :]
            return flat[:, k // 2, (k % 2) * 2048:(k % 2) * 2048 + 2048]

        def load(k):
            half = (k % 2) * 2048
            wait("sync", ss, red_tags.get(k - 2, prev_s))
            wait("sync", sv, prev_v)
            return qld.inc("sync", lambda e, k=k, half=half:
                           e.dma_start(out=xbuf[:, half:half + 2048],
                                       in_=chunk_ap(k)))

        def quant(k):
            half = (k % 2) * 2048
            wait_ts("scalar", [ld_tags[k]])
            last_sp = 0
            for piece in range(2):
                stg = htb[:, 0:1024] if piece == 0 else ZSL(0)
                wait("scalar", sv, stg_free[piece])
                sp = inc("scalar", ss, lambda e, half=half, piece=piece, stg=stg,
                         s_col=s_col: e.activation(
                             stg, xbuf[:, half + piece * 1024: half + (piece + 1) * 1024],
                             ACT.Identity, bias=stc(MAGICC), scale=stc(s_col)))
                last_sp = sp
                wait("vector", ss, sp)
                run("vector", lambda e, stg=stg: e.tensor_scalar(
                    stg, stg, -MAGIC, 1.0, ALU.add, ALU.min))
                vsync()
                stg_free[piece] = inc("vector", sv, lambda e, stg=stg, k=k,
                                      piece=piece: e.tensor_scalar(
                                          qt[:, (k % 2) * 2048 + piece * 1024:
                                             (k % 2) * 2048 + (piece + 1) * 1024],
                                          stg, -1.0, 1.0, ALU.max, ALU.mult))
            red_tags[k] = last_sp
            qv_tags[k] = stg_free[1]

        def txc(k):
            qhalf = (k % 2) * 2048
            wait("tensor", sv, qv_tags[k])
            for g in range(2):
                if width == 1024:
                    t_i = 2 * k + g
                    dst = arena_v[:, :, t_i * 128:(t_i + 1) * 128]
                else:
                    t_i = k // 2
                    ic0 = (k % 2) * 16 + g * 8
                    dst = arena_v[:, ic0:ic0 + 8, t_i * 128:(t_i + 1) * 128]
                for cc in range(8):
                    tx_bf16(qt[:, qhalf + g * 1024 + cc * 128:
                               qhalf + g * 1024 + cc * 128 + 128], cc)
                ttag = mark("tensor", st_)
                wait("vector", st_, ttag)
                ev = inc("vector", sv, lambda e, dst=dst: e.tensor_copy(
                    dst, pt_ps[:, 0:1024]))
                wait("tensor", sv, ev)

        ld_tags = {0: load(0), 1: load(1)}
        for k in range(nch):
            quant(k)
            if k + 2 < nch:
                ld_tags[k + 2] = load(k + 2)
            if k > 0:
                txc(k - 1)
        txc(nch - 1)

    quant_pass(wg_d, 32, 1024, SGC, 0, True)
    wait("scalar", sv, sv.n)
    wg_stored = []
    for dc in range(8):
        wg_stored.append(qso.inc("scalar", lambda e, dc=dc: e.dma_start(
            out=wguT_s[0, dc], in_=arena[:, dc * 4096:(dc + 1) * 4096])))
    quant_pass(wu_d, 32, 1024, SGC + 1, 32768, True)
    wait("scalar", sv, sv.n)
    for dc in range(8):
        wg_stored.append(qso.inc("scalar", lambda e, dc=dc: e.dma_start(
            out=wguT_s[1, dc], in_=arena[:, 32768 + dc * 4096: 32768 + (dc + 1) * 4096])))
    wait_ts("tensor", wg_stored[:8])
    quant_pass(wd_d, 8, 4096, SGC + 2, 0, False)
    wait("scalar", sv, sv.n)
    wd_stored = []
    for ic0 in range(0, 32, 8):
        wd_stored.append(qso.inc("scalar", lambda e, ic0=ic0: e.dma_start(
            out=wdT_s[ic0:ic0 + 8], in_=arena[:, ic0 * 1024:(ic0 + 8) * 1024])))
    x0_pre = {}
    wait("sync", ss, ss.n)
    x0_pre[0] = qld.inc("sync", lambda e: e.dma_start(
        out=xbuf.rearrange("p (q d) -> p q d", q=4),
        in_=x_d[0].rearrange("(u q p) d -> u p q d", p=128, q=4)[0]))
    if debug:
        wait_ts("scalar", wd_stored)
        for dc in range(8):
            qso.inc("scalar", lambda e, dc=dc: e.dma_start(
                out=dbg["wgT"][dc], in_=wguT_s[0, dc]))
        for ic0 in range(0, 32, 8):
            qso.inc("scalar", lambda e, ic0=ic0: e.dma_start(
                out=dbg["wdT"][ic0:ic0 + 8], in_=wdT_s[ic0:ic0 + 8]))
        qso.inc("scalar", lambda e: e.dma_start(out=dbg["mw"][0:3], in_=mw_s[0:3]))

    score_mark = [0]
    for sq in range(SEQS):
        if sq == 1 and stage < 6:
            break
        x_sup = x_d[sq].rearrange("(u q p) d -> u p q d", p=128, q=4)
        out_seq = out_d[sq].rearrange("(t p) d -> t p d", p=128)

        wait("tensor", sv, sv.n)
        wait("tensor", ss, ss.n)
        wait_ts("sync", qso.bar())
        wait("sync", st_, st_.n)
        wait_ts("scalar", qld.bar())
        wait_ts("scalar", qso.bar())
        wait_ts("scalar", bias_tkts)

        if sq == 0:
            wkv_done = wkv_pref
        else:
            wkv_done = []
            for wi, slot in ((1, 0), (2, 1)):
                for dc in range(8):
                    wkv_done.append(qld.inc("sync", lambda e, wi=wi, slot=slot, dc=dc:
                                    e.dma_start(out=WAT(slot, dc), in_=wqkvT_s[wi, dc])))

        if stage < 2 or stage >= 11:
            break
        if sq == 0:
            wait_ts("tensor", wd_stored)
        kv_free = {b: 0 for b in range(4)}
        v_free = {0: 0, 1: 0}
        xt_free_t = 0
        xb_free_t = 0
        for sup in range(NSUP):
            if sup == 0 and sq in x0_pre:
                xtag = x0_pre[sq]
            else:
                wait("sync", st_, xb_free_t)
                xtag = qld.inc("sync", lambda e, sup=sup, x_sup=x_sup: e.dma_start(
                    out=xbuf.rearrange("p (q d) -> p q d", q=4), in_=x_sup[sup]))
            wait_t("tensor", xtag)
            wait("tensor", st_, xt_free_t)
            ev_tag = {0: 0, 1: 0}
            for dc in range(8):
                bk = dc % 2
                wait("tensor", ss, ev_tag[bk])
                for tt2 in range(4):
                    tx_f32(xbuf[:, tt2 * 1024 + dc * 128: tt2 * 1024 + dc * 128 + 128],
                           tt2, bk)
                ttag = mark("tensor", st_)
                wait("scalar", st_, ttag)
                ev_tag[bk] = inc("scalar", ss, lambda e, dc=dc, bk=bk: e.activation(
                    XT(dc), XPSB[bk][:], ACT.Copy))
            wait("tensor", ss, ev_tag[1])
            xb_free_t = ttag
            wait_ts("tensor", wkv_done)
            for kc in range(8):
                b = kc % 4
                wait("tensor", ss, kv_free[b])
                for dc in range(8):
                    run("tensor", lambda e, dc=dc, kc=kc, b=b: nc.tensor.matmul(
                        s_ps[:, b * 512:(b + 1) * 512],
                        WAT(0, dc)[:, kc * 128:(kc + 1) * 128],
                        XT(dc), start=(dc == 0), stop=(dc == 7)))
                ktag = mark("tensor", st_)
                wait("scalar", st_, ktag)
                kv_free[b] = inc("scalar", ss, lambda e, kc=kc, sup=sup, b=b: e.activation(
                    KT(kc)[:, sup * 512:(sup + 1) * 512], s_ps[:, b * 512:(b + 1) * 512],
                    ACT.Identity, bias=stat[:, BKC + kc:BKC + 1 + kc]))
            for tt2 in range(4):
                for half in range(2):
                    wait("tensor", sv, v_free[half])
                    for dc in range(8):
                        run("tensor", lambda e, dc=dc, tt2=tt2, half=half: nc.tensor.matmul(
                            z_ps[:, half * 512:(half + 1) * 512],
                            XT(dc)[:, tt2 * 128:(tt2 + 1) * 128],
                            WAT(1, dc)[:, half * 512:(half + 1) * 512],
                            start=(dc == 0), stop=(dc == 7)))
                    vtag_t = mark("tensor", st_)
                    wait("vector", st_, vtag_t)
                    v_free[half] = inc("vector", sv, lambda e, tt2=tt2, half=half, sup=sup:
                                       e.tensor_copy(
                                           Vt(sup * 4 + tt2)[:, half * 512:(half + 1) * 512],
                                           z_ps[:, half * 512:(half + 1) * 512]))
            xt_free_t = vtag_t

        if debug and sq == 0:
            wait("scalar", sv, sv.n)
            wait("scalar", ss, ss.n)
            for kc in range(8):
                qso.inc("scalar", lambda e, kc=kc: e.dma_start(
                    out=dbg["kt"][kc], in_=KT(kc)))
            for vt in range(NT):
                qso.inc("scalar", lambda e, vt=vt: e.dma_start(
                    out=dbg["v"][vt], in_=Vt(vt)))

        if stage < 3:
            break
        wait("sync", st_, st_.n)
        wq_done = []
        for dc in range(8):
            wq_done.append(qld.inc("sync", lambda e, dc=dc:
                           e.dma_start(out=WAT(0, dc), in_=wqkvT_s[0, dc])))
        wait_ts("tensor", wq_done)

        zsl_free = {0: (None, 0, 0), 1: (None, 0, 0)}
        xb_free = {0: (0, xb_free_t, 0), 1: (0, 0, 0)}
        xt_free2 = xt_free_t
        sc_free = {b: 0 for b in range(6)}
        sc_free_v = {b: 0 for b in range(6)}
        ptev_bank = {b: 0 for b in range(6)}
        ptsl_free = {0: 0, 1: 0}
        psl_free = {0: 0, 1: 0}
        zpv_tag = 0
        sq_free_s = 0
        XB = lambda sp: (xbuf if sp % 2 == 0 else wbuf[:, 8192:16384].bitcast(F32))
        xloads = {}

        def x_load(sp):
            if sp >= NSUP:
                return
            fr = xb_free[sp % 2]
            wait("sync", sv, fr[0])
            wait("sync", st_, fr[1])
            xloads[sp] = qld.inc("sync", lambda e, sp=sp, x_sup=x_sup: e.dma_start(
                out=XB(sp).rearrange("p (q d) -> p q d", q=4), in_=x_sup[sp]))

        x_load(0)
        for sup in range(NSUP):
            x_load(sup + 1)
            xb = XB(sup)
            wait_t("tensor", xloads[sup])
            wait("tensor", st_, xt_free2)
            ev_tag = {0: 0, 1: 0}
            for dc in range(8):
                bk = dc % 2
                wait("tensor", ss, ev_tag[bk])
                for tt2 in range(4):
                    tx_f32(xb[:, tt2 * 1024 + dc * 128: tt2 * 1024 + dc * 128 + 128],
                           tt2, bk)
                ttag = mark("tensor", st_)
                wait("scalar", st_, ttag)
                ev_tag[bk] = inc("scalar", ss, lambda e, dc=dc, bk=bk: e.activation(
                    XT(dc), XPSB[bk][:], ACT.Copy))
            wait("tensor", ss, ev_tag[1])
            xb_free_t = ttag
            wait("gpsimd", st_, xb_free_t)
            wait_t("gpsimd", xloads[sup])
            bvtag = qgl.inc("gpsimd", lambda e, xb=xb: e.dma_start(
                out=xb.rearrange("p (q d) -> p q d", q=4),
                in_=bv_d[None, None, :].to_broadcast([128, 4, D]),
                accum_op=ALU.add))
            for kc in range(8):
                b = kc % 4
                wait("tensor", ss, kv_free[b])
                for dc in range(8):
                    run("tensor", lambda e, dc=dc, kc=kc, b=b: nc.tensor.matmul(
                        s_ps[:, b * 512:(b + 1) * 512],
                        WAT(0, dc)[:, kc * 128:(kc + 1) * 128],
                        XT(dc), start=(dc == 0), stop=(dc == 7)))
                qtag = mark("tensor", st_)
                wait("scalar", st_, qtag)
                kv_free[b] = inc("scalar", ss, lambda e, kc=kc, b=b: e.activation(
                    QT(kc), s_ps[:, b * 512:(b + 1) * 512],
                    ACT.Identity, bias=stat[:, BQC + kc:BQC + 1 + kc]))
            q_done_s = ss.n

            for sub in range(4):
                i = sup * 4 + sub
                nfull = i // 4
                dw = (i % 4 + 1) * 128
                chunks = [(c * 512, 512, False) for c in range(nfull)]
                chunks.append((nfull * 512, dw, True))
                nch = len(chunks)
                waves = [chunks] if nch <= 6 else [chunks[:6], chunks[6:]]
                wait("tensor", ss, q_done_s)
                first_pv = True
                lcols = 0
                last_exp = 0
                for wv_i, wave in enumerate(waves):
                    for ci, (off, w, isdiag) in enumerate(wave):
                        b = ci
                        wait("tensor", ss, sc_free[b])
                        wait("tensor", sv, sc_free_v[b])
                        wait("tensor", sv, ptev_bank[b])
                        for kc in range(8):
                            run("tensor", lambda e, kc=kc, off=off, w=w, b=b, sub=sub:
                                nc.tensor.matmul(
                                    s_ps[:, b * 512: b * 512 + w],
                                    QT(kc)[:, sub * 128:(sub + 1) * 128],
                                    KT(kc)[:, off:off + w],
                                    start=(kc == 0), stop=(kc == 7)))
                        stag = mark("tensor", st_)
                        score_mark[0] = stag
                        wait("vector", st_, stag)
                        if isdiag:
                            run("vector", lambda e, b=b, w=w: e.tensor_tensor(
                                s_ps[:, b * 512 + w - 128: b * 512 + w],
                                s_ps[:, b * 512 + w - 128: b * 512 + w],
                                mask_sb[:], ALU.add))
                            vsync()
                        sc_free_v[b] = inc("vector", sv, lambda e, b=b, w=w, ci=ci:
                                           e.reduce_max(stc(CMAXC + ci),
                                                        s_ps[:, b * 512: b * 512 + w], AX))
                    nw = len(wave)
                    vsync()
                    if wv_i == 0:
                        run("vector", lambda e, nw=nw: e.reduce_max(
                            stc(MISC + 0), stat[:, CMAXC:CMAXC + nw], AX))
                    else:
                        wait("vector", ss, last_exp)
                        run("vector", lambda e, nw=nw: e.reduce_max(
                            stc(MISC + 2), stat[:, CMAXC:CMAXC + nw], AX))
                        vsync()
                        run("vector", lambda e: e.tensor_tensor(
                            stc(MISC + 3), stc(MISC + 0), stc(MISC + 2), ALU.max))
                        vsync()
                        run("vector", lambda e: e.tensor_tensor(
                            stc(MISC + 4), stc(MISC + 0), stc(MISC + 3), ALU.subtract))
                        vsync()
                        run("vector", lambda e: e.tensor_scalar(
                            stc(MISC + 4), stc(MISC + 4), 1.0 / 32.0, 0.0, ALU.mult, ALU.add))
                        run("vector", lambda e: e.tensor_copy(stc(MISC + 0), stc(MISC + 3)))
                    vsync()
                    run("vector", lambda e: e.tensor_scalar(
                        stc(MISC + 1), stc(MISC + 0), -1.0 / 32.0, 0.0, ALU.mult, ALU.add))
                    if wv_i == 1:
                        vt_ = mark("vector", sv)
                        wait("scalar", sv, vt_)
                        st2 = inc("scalar", ss, lambda e: e.activation(
                            stc(MISC + 4), stc(MISC + 4), ACT.Exp))
                        wait("vector", ss, st2)
                        wait("vector", st_, zpv_tag)
                        run("vector", lambda e: e.tensor_scalar(
                            HT_, z_ps[:], stc(MISC + 4), 0.0, ALU.mult, ALU.add))
                        run("vector", lambda e, lcols=lcols: e.tensor_scalar(
                            stat[:, LPARTC:LPARTC + lcols], stat[:, LPARTC:LPARTC + lcols],
                            stc(MISC + 4), 0.0, ALU.mult, ALU.add))
                        zres = mark("vector", sv)
                        wait("tensor", sv, zres)
                        first_pv = True
                    mv_tag = zres if wv_i == 1 else mark("vector", sv)
                    wait("scalar", sv, mv_tag)
                    for ci, (off, w, isdiag) in enumerate(wave):
                        b = ci
                        ns = w // 128
                        pslot = ci % 2
                        wait("scalar", st_, psl_free[pslot])
                        sc_free[b] = inc("scalar", ss, lambda e, b=b, w=w, pslot=pslot,
                                         lci=lcols: e.activation(
                                             PSL(pslot)[:, 0:w], s_ps[:, b * 512: b * 512 + w],
                                             ACT.Exp, bias=stc(MISC + 1), scale=1.0 / 32.0,
                                             accum_out=stc(LPARTC + lci)))
                        last_exp = sc_free[b]
                        lcols += 1
                        wait("tensor", ss, sc_free[b])
                        pbank = s_ps.bitcast(BF16)[:, b * 1024:(b + 1) * 1024]
                        for j in range(ns):
                            run("tensor", lambda e, a=PSL(pslot)[:, j * 128:(j + 1) * 128],
                                j=j, pbank=pbank: nc.tensor.transpose(
                                    pbank[:, j * 128:(j + 1) * 128], a, id_sb[:]))
                        psl_free[pslot] = mark("tensor", st_)
                        ptslot = ci % 2
                        wait("vector", st_, psl_free[pslot])
                        wait("vector", st_, ptsl_free[ptslot])
                        ptev = inc("vector", sv, lambda e, ns=ns, ptslot=ptslot, pbank=pbank:
                                   e.tensor_copy(PTSL(ptslot)[:, 0:ns * 128],
                                                 pbank[:, 0:ns * 128]))
                        ptev_bank[b] = ptev
                        wait("tensor", sv, ptev)
                        wait("tensor", ss, sq_free_s)
                        last_chunk = (ci == len(wave) - 1)
                        for j in range(ns):
                            s_tile = off // 128 + j
                            lastj = last_chunk and (j == ns - 1)
                            for half in range(2):
                                run("tensor", lambda e, j=j, half=half, s_tile=s_tile,
                                    ptslot=ptslot, fp=first_pv, lastj=lastj: nc.tensor.matmul(
                                        z_ps[:, half * 512:(half + 1) * 512],
                                        PTSL(ptslot)[:, j * 128:(j + 1) * 128],
                                        Vt(s_tile)[:, half * 512:(half + 1) * 512],
                                        start=fp, stop=lastj))
                            first_pv = False
                        ptsl_free[ptslot] = mark("tensor", st_)
                    zpv_tag = st_.n
                wait("vector", st_, zpv_tag)
                wait("vector", ss, last_exp)
                wait_t("vector", bvtag)
                if len(waves) == 2:
                    run("vector", lambda e: e.scalar_tensor_tensor(
                        HT_, z_ps[:], 1.0, HT_, ALU.mult, ALU.add))
                    vsync()
                run("vector", lambda e, lcols=lcols: e.reduce_sum(
                    stc(MISC + 5), stat[:, LPARTC:LPARTC + lcols], AX))
                vsync()
                run("vector", lambda e: e.reciprocal(stc(MISC + 6), stc(MISC + 5)))
                vsync()
                zslot = i % 2
                wait_t("vector", zsl_free[zslot][0])
                wait("vector", ss, zsl_free[zslot][1])
                zsrc = HT_ if len(waves) == 2 else z_ps[:]
                xbt = xb[:, sub * 1024:(sub + 1) * 1024]
                ztag = inc("vector", sv, lambda e, zslot=zslot, zsrc=zsrc, xbt=xbt:
                           e.scalar_tensor_tensor(
                               ZSL(zslot), zsrc, stc(MISC + 6), xbt,
                               ALU.mult, ALU.add, accum_out=stc(MISC + 7)))
                wait("scalar", sv, ztag)
                sqtag = inc("scalar", ss, lambda e, zslot=zslot: e.activation(
                    z_ps[:], ZSL(zslot), ACT.Square, accum_out=stc(MISC + 8)))
                sq_free_s = sqtag
                wait("vector", sv, ztag)
                wait("vector", ss, sqtag)
                run("vector", lambda e, i=i: e.tensor_scalar(
                    stc(M1C + i), stc(MISC + 7), 1.0 / D, 0.0, ALU.mult, ALU.add))
                vsync()
                run("vector", lambda e, i=i: e.tensor_tensor(
                    stc(MISC + 9), stc(M1C + i), stc(M1C + i), ALU.mult))
                run("vector", lambda e: e.tensor_scalar(
                    stc(MISC + 8), stc(MISC + 8), 1.0 / D, 0.0, ALU.mult, ALU.add))
                vsync()
                vstat = inc("vector", sv, lambda e, i=i: e.tensor_tensor(
                    stc(V1C + i), stc(MISC + 8), stc(MISC + 9), ALU.subtract))
                wait("scalar", sv, ztag)
                stotag = qso.inc("scalar", lambda e, i=i, zslot=zslot: e.dma_start(
                    out=z_sd[i], in_=ZSL(zslot)))
                zsl_free[zslot] = (stotag, sqtag, ztag)
            xb_free[sup % 2] = (sv.n, xb_free_t, 0)
            xt_free2 = st_.n
        attn_v = sv.n

        if debug and sq == 0:
            wait_ts("scalar", qso.bar())
            for tt in range(NT):
                qso.inc("scalar", lambda e, tt=tt: e.dma_start(
                    out=dbg["z"][tt], in_=z_sd[tt]))

        if stage < 4:
            break
        wait("sync", st_, score_mark[0])
        gu_done = []
        for dc in range(8):
            gu_done.append(qld.inc("sync", lambda e, dc=dc: e.dma_start(
                out=WGT(dc), in_=wguT_s[0, dc])))
        wait("sync", st_, st_.n)
        pre_zld = {}
        for t in range(2):
            wait_t("sync", zsl_free[t][0])
            wait("sync", ss, zsl_free[t][1])
            pre_zld[t] = qld.inc("sync", lambda e, t=t: e.dma_start(
                out=ZSL(t), in_=z_sd[t]))
        for dc in range(8):
            gu_done.append(qld.inc("sync", lambda e, dc=dc: e.dma_start(
                out=WUT(dc), in_=wguT_s[1, dc])))
        wait("vector", sv, attn_v)
        run("vector", lambda e: e.tensor_scalar(
            stat[:, MISC + 0:MISC + 0 + NT], stat[:, V1C:V1C + NT], EPS, 1.0,
            ALU.add, ALU.mult))
        vtag = mark("vector", sv)
        wait("scalar", sv, vtag)
        stag = inc("scalar", ss, lambda e: e.activation(
            stat[:, R1C:R1C + NT], stat[:, MISC + 0:MISC + 0 + NT], ACT.Sqrt))
        wait("vector", ss, stag)
        r1tag = inc("vector", sv, lambda e: e.reciprocal(
            stat[:, R1C:R1C + NT], stat[:, R1C:R1C + NT]))
        wait("vector", sv, r1tag)

        zld_free = {0: (0, 0), 1: (0, 0)}
        PSLOT = [s_ps[:, b * 512:(b + 1) * 512] for b in range(5)] + \
                [z_ps[:, 0:512], z_ps[:, 512:1024]]
        gu_pool = [0] * 7
        gu_pool_s = [0] * 7
        t1_free = {0: 0, 1: 0}
        prev_pev = 0
        MP = lambda t: MISC + 24 * (t % 2)
        PINTB = lambda t: (PINT if t % 2 == 0 else qt[:, 0:4096])
        PSTGB = lambda t: (PSTG if t % 2 == 0 else wbuf[:, 0:4096])
        gu_last_mm = [0]
        chain_tag = {}
        pq_tags = {}
        pass2_tag = {}
        hp2_tag = {}
        htev_tag = {}
        pstg_stores = []

        def gu_chain(t):
            zslot = t % 2
            mp = MP(t)
            if t < 2:
                zldt = pre_zld[t]
            else:
                wait("sync", sv, zld_free[zslot][0])
                wait("sync", ss, zld_free[zslot][1])
                zldt = qld.inc("sync", lambda e, t=t, zslot=zslot: e.dma_start(
                    out=ZSL(zslot), in_=z_sd[t]))
            wait_t("vector", zldt)
            run("vector", lambda e, zslot=zslot, t=t: e.tensor_scalar(
                ZSL(zslot), ZSL(zslot), stc(M1C + t), 1.0, ALU.subtract, ALU.mult))
            vsync()
            run("vector", lambda e, zslot=zslot, mp=mp: e.reduce_max(
                stc(mp + 0), ZSL(zslot), AX, apply_absolute_value=True))
            vsync()
            run("vector", lambda e, t=t, mp=mp: e.tensor_tensor(
                stc(mp + 1), stc(mp + 0), stc(R1C + t), ALU.mult))
            vsync()
            run("vector", lambda e, mp=mp: e.tensor_scalar(
                stc(mp + 1), stc(mp + 1), 1e-5, 1.0 / 127.0, ALU.max, ALU.mult))
            vsync()
            run("vector", lambda e, mp=mp: e.reciprocal(stc(mp + 2), stc(mp + 1)))
            vsync()
            run("vector", lambda e, t=t, mp=mp: e.tensor_tensor(
                stc(mp + 3), stc(mp + 2), stc(R1C + t), ALU.mult))
            run("vector", lambda e, mp=mp: e.tensor_tensor(
                stc(mp + 4), stc(MGC), stc(mp + 1), ALU.mult))
            vq = inc("vector", sv, lambda e, mp=mp: e.tensor_tensor(
                stc(mp + 5), stc(MGC + 1), stc(mp + 1), ALU.mult))
            chain_tag[t] = (vq, zslot)

        def gu_hint(t):
            nonlocal prev_pev
            vq, zslot = chain_tag[t]
            mp = MP(t)
            wait("scalar", sv, vq)
            wait("scalar", sv, gu_pool[5])
            wait("scalar", sv, gu_pool[6])
            hp1 = inc("scalar", ss, lambda e, zslot=zslot, mp=mp: e.activation(
                z_ps[:], ZSL(zslot), ACT.Identity, bias=stc(MAGICC), scale=stc(mp + 3)))
            wait("vector", ss, hp1)
            hp2 = inc("vector", sv, lambda e: e.tensor_scalar(
                HINT, z_ps[:], -MAGIC, 1.0, ALU.add, ALU.mult))
            hp2_tag[t] = hp2
            zld_free[zslot] = (hp2, hp1)
            wait("tensor", sv, hp2)
            wait("tensor", sv, prev_pev)
            for dc in range(8):
                tx_bf16(HINT[:, dc * 128:(dc + 1) * 128], dc)
            htx = mark("tensor", st_)
            wait("vector", st_, htx)
            htev_tag[t] = inc("vector", sv, lambda e: e.tensor_copy(
                wbuf[:, 9216:10240], pt_ps[:, 0:1024]))

        def gu_chunks(t):
            nonlocal prev_pev
            mp = MP(t)
            wait("tensor", sv, htev_tag[t])
            prev = pq_tags.get(t - 1)

            for c in range(8):
                sg = (2 * c) % 7
                su = (2 * c + 1) % 7
                wait("tensor", sv, gu_pool[sg])
                wait("tensor", ss, gu_pool_s[sg])
                wait("tensor", sv, gu_pool[su])
                wait("tensor", ss, gu_pool_s[su])
                for dc in range(8):
                    wait_ts("tensor", [gu_done[dc]])
                    run("tensor", lambda e, dc=dc, c=c, sg=sg: nc.tensor.matmul(
                        PSLOT[sg], HTT(dc), WGT(dc)[:, c * 512:(c + 1) * 512],
                        start=(dc == 0), stop=(dc == 7)))
                    wait_ts("tensor", [gu_done[8 + dc]])
                    run("tensor", lambda e, dc=dc, c=c, su=su: nc.tensor.matmul(
                        PSLOT[su], HTT(dc), WUT(dc)[:, c * 512:(c + 1) * 512],
                        start=(dc == 0), stop=(dc == 7)))
                gutag = mark("tensor", st_)
                gu_last_mm[0] = gutag
                t1s = c % 2
                wait("scalar", st_, gutag)
                wait("scalar", sv, t1_free[t1s])
                gu_pool_s[sg] = inc("scalar", ss, lambda e, sg=sg, t1s=t1s, mp=mp:
                                    e.activation(T1(t1s), PSLOT[sg], ACT.Silu,
                                                 scale=stc(mp + 4)))
                wait("vector", ss, gu_pool_s[sg])
                if prev is not None:
                    wait("vector", ss, prev[c // 2])
                ptag = inc("vector", sv, lambda e, su=su, c=c, t1s=t1s: e.tensor_tensor(
                    PBUF[:, c * 512:(c + 1) * 512], T1(t1s), PSLOT[su], ALU.mult))
                gu_pool[su] = ptag
                gu_pool[sg] = ptag
                gu_pool_s[su] = gu_pool_s[sg]
                t1_free[t1s] = ptag
                wait("vector", sv, ptag)
                run("vector", lambda e, c=c, mp=mp: e.reduce_max(
                    stc(mp + 8 + c), PBUF[:, c * 512:(c + 1) * 512], AX,
                    apply_absolute_value=True))

        def gu_finalize(t):
            mp = MP(t)
            vsync()
            run("vector", lambda e, mp=mp: e.reduce_max(
                stc(mp + 16), stat[:, mp + 8:mp + 16], AX))
            vsync()
            run("vector", lambda e, mp=mp: e.tensor_tensor(
                stc(mp + 17), stc(mp + 16), stc(mp + 5), ALU.mult))
            vsync()
            run("vector", lambda e, mp=mp: e.tensor_scalar(
                stc(mp + 17), stc(mp + 17), 1e-5, 1.0 / 127.0, ALU.max, ALU.mult))
            vsync()
            run("vector", lambda e, mp=mp: e.reciprocal(stc(mp + 18), stc(mp + 17)))
            vsync()
            run("vector", lambda e, mp=mp: e.tensor_tensor(
                stc(mp + 19), stc(mp + 18), stc(mp + 5), ALU.mult))
            vq2 = inc("vector", sv, lambda e, t=t, mp=mp: e.tensor_tensor(
                stc(GAMC + t), stc(mp + 17), stc(MGC + 2), ALU.mult))
            wait("scalar", sv, vq2)
            if t + 1 in hp2_tag:
                wait("scalar", sv, hp2_tag[t + 1])
            tags = []
            pv_last = 0
            for piece in range(4):
                wait("scalar", sv, pv_last)
                pp1 = inc("scalar", ss, lambda e, piece=piece, mp=mp: e.activation(
                    z_ps[:], PBUF[:, piece * 1024:(piece + 1) * 1024], ACT.Identity,
                    bias=stc(MAGICC), scale=stc(mp + 19)))
                tags.append(pp1)
                wait("vector", ss, pp1)
                pv_last = inc("vector", sv, lambda e, piece=piece, t=t: e.tensor_scalar(
                    PINTB(t)[:, piece * 1024:(piece + 1) * 1024], z_ps[:], -MAGIC, 1.0,
                    ALU.add, ALU.mult))
            pq_tags[t] = tags
            pass2_tag[t] = pv_last
            gu_pool[5] = pv_last
            gu_pool[6] = pv_last
            gu_pool_s[5] = gu_pool_s[6] = ss.n

        def gu_ptx(t):
            nonlocal prev_pev
            if t < 0:
                return
            wait("tensor", sv, pass2_tag[t])
            pstg_last = pstg_stores[-2] if len(pstg_stores) >= 2 else None
            for ic0 in range(0, 32, 8):
                for cc in range(8):
                    tx_bf16(PINTB(t)[:, (ic0 + cc) * 128:(ic0 + cc + 1) * 128], cc)
                ptx = mark("tensor", st_)
                wait("vector", st_, ptx)
                wait_t("vector", pstg_last)
                prev_pev = inc("vector", sv, lambda e, ic0=ic0, t=t: e.tensor_copy(
                    PSTGB(t)[:, ic0 * 128:(ic0 + 8) * 128],
                    pt_ps[:, 0:1024]))
                wait("tensor", sv, prev_pev)
            wait("scalar", sv, prev_pev)
            pstg_stores.append(qso.inc("scalar", lambda e, t=t: e.dma_start(
                out=pint_s[t], in_=PSTGB(t))))

        gu_chain(0)
        gu_hint(0)
        for tt in range(NT):
            if tt + 1 < NT:
                gu_chain(tt + 1)
            gu_chunks(tt)
            gu_ptx(tt - 1)
            if tt + 1 < NT:
                gu_hint(tt + 1)
            gu_finalize(tt)
        gu_ptx(NT - 1)

        if debug and sq == 0:
            wait_ts("scalar", qso.bar())
            for tt in range(NT):
                qso.inc("scalar", lambda e, tt=tt: e.dma_start(
                    out=dbg["pint"][tt], in_=pint_s[tt]))

        if stage < 5:
            break
        wait_ts("sync", pstg_stores[-2:])
        pld0_t = qld.inc("sync", lambda e: e.dma_start(out=PTLD(0), in_=pint_s[0]))
        if sq + 1 < SEQS:
            wait("sync", ss, pq_tags[NT - 1][-1])
            x0_pre[sq + 1] = qld.inc("sync", lambda e, sq=sq: e.dma_start(
                out=xbuf.rearrange("p (q d) -> p q d", q=4),
                in_=x_d[sq + 1].rearrange("(u q p) d -> u p q d", p=128, q=4)[0]))
        wait("sync", st_, gu_last_mm[0])
        wdld_h1 = []
        wdld = []
        for ic0 in range(0, 32, 8):
            tk = qld.inc("sync", lambda e, ic0=ic0: e.dma_start(
                out=arena[:, ic0 * 1024:(ic0 + 8) * 1024], in_=wdT_s[ic0:ic0 + 8]))
            (wdld_h1 if ic0 < 16 else wdld).append(tk)
        wait_ts("tensor", wdld_h1[:1])
        ptld_free = {0: 0, 1: 0}
        yfree = {0: 0, 1: 0}
        ofree = {0: None, 1: None}
        zld2_free = {0: (0, 0), 1: (0, 0)}
        for tt in range(NT):
            b = tt % 2
            if tt == 0:
                pldt = pld0_t
            else:
                wait("sync", st_, ptld_free[b])
                pldt = qld.inc("sync", lambda e, tt=tt, b=b: e.dma_start(
                    out=PTLD(b), in_=pint_s[tt]))
            zslot = tt % 2
            wait("sync", sv, zld2_free[zslot][0])
            wait("sync", ss, zld2_free[zslot][1])
            zldt = qld.inc("sync", lambda e, tt=tt, zslot=zslot: e.dma_start(
                out=ZSL(zslot), in_=z_sd[tt]))
            wait_t("tensor", pldt)
            wait("tensor", sv, yfree[b])
            for ic in range(32):
                if ic == 8:
                    wait_ts("tensor", wdld_h1[1:])
                elif ic == 16:
                    wait_ts("tensor", wdld[:1])
                elif ic == 24:
                    wait_ts("tensor", wdld[1:])
                for half in range(2):
                    run("tensor", lambda e, ic=ic, half=half, b=b: nc.tensor.matmul(
                        s_ps[:, b * 1024 + half * 512: b * 1024 + (half + 1) * 512],
                        PTLD(b)[:, ic * 128:(ic + 1) * 128],
                        WDT(ic)[:, half * 512:(half + 1) * 512],
                        start=(ic == 0), stop=(ic == 31)))
            ytag = mark("tensor", st_)
            wait_t("vector", zldt)
            httag = inc("vector", sv, lambda e, zslot=zslot, tt=tt: e.tensor_scalar(
                HT_, ZSL(zslot), stc(M1C + tt), stc(R1C + tt), ALU.subtract, ALU.mult))
            ob = tt % 2
            wait_t("vector", ofree[ob])
            wait("vector", st_, ytag)
            wait("vector", sv, httag)
            z2tag = inc("vector", sv, lambda e, b=b, tt=tt, ob=ob: e.scalar_tensor_tensor(
                OBUF(ob), s_ps[:, b * 1024:(b + 1) * 1024], stc(GAMC + tt), HT_,
                ALU.mult, ALU.add, accum_out=stc(MISC + 0)))
            zld2_free[zslot] = (httag, 0)
            yfree[b] = z2tag
            wait("scalar", sv, z2tag)
            sq2 = inc("scalar", ss, lambda e, ob=ob: e.activation(
                z_ps[:], OBUF(ob), ACT.Square, accum_out=stc(MISC + 1)))
            wait("vector", sv, z2tag)
            wait("vector", ss, sq2)
            run("vector", lambda e: e.tensor_scalar(
                stc(MISC + 2), stc(MISC + 0), 1.0 / D, 0.0, ALU.mult, ALU.add))
            vsync()
            run("vector", lambda e: e.tensor_tensor(
                stc(MISC + 3), stc(MISC + 2), stc(MISC + 2), ALU.mult))
            run("vector", lambda e: e.tensor_scalar(
                stc(MISC + 1), stc(MISC + 1), 1.0 / D, EPS, ALU.mult, ALU.add))
            vsync()
            v2tag = inc("vector", sv, lambda e: e.tensor_tensor(
                stc(MISC + 4), stc(MISC + 1), stc(MISC + 3), ALU.subtract))
            wait("scalar", sv, v2tag)
            sqr2 = inc("scalar", ss, lambda e: e.activation(
                stc(MISC + 5), stc(MISC + 4), ACT.Sqrt))
            wait("vector", ss, sqr2)
            run("vector", lambda e: e.reciprocal(stc(MISC + 6), stc(MISC + 5)))
            vsync()
            otag = inc("vector", sv, lambda e, ob=ob: e.tensor_scalar(
                OBUF(ob), OBUF(ob), stc(MISC + 2), stc(MISC + 6), ALU.subtract, ALU.mult))
            wait("scalar", sv, otag)
            ofree[ob] = qso.inc("scalar", lambda e, tt=tt, ob=ob, out_seq=out_seq:
                                e.dma_start(out=out_seq[tt], in_=OBUF(ob)))
            ptld_free[b] = st_.n

    wait_ts("sync", qso.bar())
    drain_queues()
    return _finish(nc, ctx, block_emit)


def _consts():
    ident = np.eye(128, dtype=ml_dtypes.bfloat16)
    mask = np.zeros((128, 128), dtype=np.float32)
    for p in range(128):
        mask[p, p + 1:] = NEG
    return ident, mask


def kernel(**inputs):
    x = np.asarray(inputs["L0_attn_x"], dtype=np.float32)
    ident, mask = _consts()
    base = dict(
        wq=np.asarray(inputs["L0_attn_Wq"], dtype=np.float32),
        wk=np.asarray(inputs["L0_attn_Wk"], dtype=np.float32),
        wv=np.asarray(inputs["L0_attn_Wv"], dtype=np.float32),
        bq=np.asarray(inputs["L0_attn_bq"], dtype=np.float32),
        bk=np.asarray(inputs["L0_attn_bk"], dtype=np.float32),
        bv=np.asarray(inputs["L0_attn_bv"], dtype=np.float32),
        wg=np.asarray(inputs["L0_ffn_w_gate"], dtype=np.float32),
        wu=np.asarray(inputs["L0_ffn_w_up"], dtype=np.float32),
        wd=np.asarray(inputs["L0_ffn_w_down"], dtype=np.float32),
        ident=ident, identf=np.eye(128, dtype=np.float32), mask=mask,
        magicv=np.full((128, 1), MAGIC, dtype=np.float32),
    )
    nc = build_program(debug=False)
    in_maps = []
    for c in range(8):
        m = dict(base)
        m["x"] = np.ascontiguousarray(x[2 * c:2 * c + 2])
        in_maps.append(m)
    res = run_bass_kernel_spmd(nc, in_maps, core_ids=list(range(8)))
    out = np.concatenate([np.asarray(r["out"]) for r in res.results], axis=0)
    return out.reshape(16, 4096, 1024).astype(np.float32, copy=False)
